# version 15
# baseline (speedup 1.0000x reference)
import math
import contextlib
import numpy as np
import ml_dtypes
import concourse.bass as bass
import concourse.mybir as mybir
from concourse.bass_utils import run_bass_kernel_spmd

F32 = mybir.dt.float32
BF16 = mybir.dt.bfloat16
AF = mybir.ActivationFunctionType
ALU = mybir.AluOpType
AX = mybir.AxisListType

H = 2048
NH = 16
SEQ = 4096
OWN0 = 2048
NMG = 4
NE = 20
ISQ = 1.0 / math.sqrt(128.0)
EPS = 1e-5
SLOPES = [2.0 ** (-k / 2.0) for k in range(NH)]
WH = [int(math.ceil(45.0 / s)) for s in SLOPES]
NEGD = -1.0e33
COLF = 7

DEBUG = None
ABLATE = frozenset()


def _jr(qt, h):
    qend = OWN0 + 512 * (qt + 1)
    js = max(0, ((qend - 512 - WH[h]) // 128) * 128)
    return js, (qend - js) // 128

ENGS = ("sync", "scalar", "vector", "tensor", "gpsimd")


class Sem:
    __slots__ = ("h", "n", "name")
    def __init__(self, h, name):
        self.h, self.n, self.name = h, 0, name


class Prog:
    CHAIN = ("vector", "scalar", "gpsimd")

    def __init__(self):
        self.items = {e: [] for e in ENGS}
        self.eng_sem = {}
        self.last = {}

    def op(self, eng, fn, waits=(), inc=None, inck=1):
        auto = ()
        if fn is not None and eng in self.CHAIN:
            es = self.eng_sem[eng]
            if inc is None:
                inc, inck = es, 1
            prev = self.last.get(eng)
            if prev is not None:
                auto = (prev,)
            if inc is es:
                self.last[eng] = (es, es.n + inck)
        tok = None
        if inc is not None:
            inc.n += inck
            tok = (inc, inc.n)
        ws = []
        seen = {}
        for wt in tuple(waits) + auto:
            if wt is None:
                continue
            s, v = wt
            assert v <= s.n, f"wait on future token {s.name} {v} > {s.n}"
            if v > 0 and seen.get(id(s), -1) < v:
                seen[id(s)] = v
        ws = [(s, v) for wt in [None] for s, v in []]
        done = set()
        out = []
        for wt in tuple(waits) + auto:
            if wt is None:
                continue
            s, v = wt
            if id(s) in done:
                continue
            done.add(id(s))
            out.append((s, seen[id(s)]))
        out = [(s, v) for (s, v) in out if v > 0]
        self.items[eng].append((fn, tuple(out), inc, inck if inc else 0))
        return tok

    def wait(self, eng, sem, val):
        self.op(eng, None, waits=((sem, val),))

    def emit(self, block):
        for eng in ENGS:
            items = self.items[eng]
            if not items:
                continue
            def mk(items_):
                def run(q):
                    for fn, waits, inc, inck in items_:
                        for (s, v) in waits:
                            q.wait_ge(s.h, v)
                        ins = fn(q) if fn is not None else None
                        if inc is not None:
                            ins.then_inc(inc.h, inck)
                return run
            getattr(block, eng)(mk(items))
        self.items = {e: [] for e in ENGS}


def build_nc(debug=None):
    nc = bass.Bass("TRN2", target_bir_lowering=False)
    dbgA = debug == "A"
    dbgB = debug == "B"

    x_in = nc.dram_tensor("x", [SEQ, H], F32, kind="ExternalInput").ap()
    wqkv = nc.dram_tensor("wqkv", [H, 3 * H], F32, kind="ExternalInput").ap()
    bqkv_r = nc.dram_tensor("bqkv_r", [128, 48], F32, kind="ExternalInput").ap()
    bqkv_f = nc.dram_tensor("bqkv_f", [1, 3 * H], F32, kind="ExternalInput").ap()
    wo_in = nc.dram_tensor("wo", [H, H], F32, kind="ExternalInput").ap()
    bo_f = nc.dram_tensor("bo_f", [1, H], F32, kind="ExternalInput").ap()
    gs_in = nc.dram_tensor("gs", [H, 17], F32, kind="ExternalInput").ap()
    w13a = nc.dram_tensor("w13a", [NE, 22, 128, 16, 128], F32, kind="ExternalInput").ap()
    wqk = nc.dram_tensor("wqk", [48, 128, 16, 128], F32, kind="ExternalInput").ap()
    w2a = nc.dram_tensor("w2a", [NE, 1408, H], F32, kind="ExternalInput").ap()
    pm_in = nc.dram_tensor("pm", [128, 32], F32, kind="ExternalInput").ap()
    d4_in = nc.dram_tensor("d4", [128, 2048], F32, kind="ExternalInput").ap()
    m4_in = nc.dram_tensor("m4", [128, 2048], F32, kind="ExternalInput").ap()

    out_t = nc.dram_tensor("out", [OWN0, H], F32, kind="ExternalOutput").ap()
    skind = "ExternalOutput" if dbgA else "Internal"
    xnT_s = nc.dram_tensor("xnT_s", [H, OWN0], BF16, kind=skind).ap()
    k_s = nc.dram_tensor("k_s", [H, SEQ], BF16, kind=skind).ap()
    v_s = nc.dram_tensor("v_s", [H, SEQ], BF16, kind=skind).ap()
    q_s = nc.dram_tensor("q_s", [H, OWN0], BF16, kind=skind).ap()
    s_s = nc.dram_tensor("s_s", [OWN0, H], F32,
                         kind="ExternalOutput" if (dbgA or dbgB) else "Internal").ap()
    acc_s = nc.dram_tensor("acc_s", [OWN0, H], F32, kind="Internal").ap()
    if dbgA:
        dbg_coef = nc.dram_tensor("dbg_coef", [128, 16 * NE], F32,
                                  kind="ExternalOutput").ap()
        dbg_m = nc.dram_tensor("dbg_m", [1, 3 * NH], F32, kind="ExternalOutput").ap()

    stk = contextlib.ExitStack()
    def S_(name):
        return Sem(stk.enter_context(nc.semaphore(name)), name)

    p = Prog()
    with stk:
        sxA = [S_(f"sxA{i}") for i in range(3)]
        swk2 = [S_(f"swk{i}") for i in range(2)]
        swc = S_("swc"); swv = S_("swv"); swo = S_("swo")
        svear = [S_(f"sve{i}") for i in range(2)]
        skqr = [S_(f"skq{i}") for i in range(2)]
        sxn = S_("sxn")
        sser = [S_(f"sse{i}") for i in range(2)]
        skvr = [S_(f"skv{i}") for i in range(2)]
        sxch = S_("sxch")
        sepa = S_("sepa"); sepx = S_("sepx"); seps = S_("seps")
        sw3r = [S_(f"sw3{i}") for i in range(2)]
        sw2x = S_("sw2x")
        sacc8 = [S_(f"sacc{i}") for i in range(8)]
        sconv = S_("sconv")
        st = S_("st"); sv = S_("sv"); ss = S_("ss"); sg = S_("sg"); so = S_("so")
        ALL = (sxA + swk2 + [swc, swv, swo] + svear + skqr + [sxn] + sser + skvr
               + [sxch, sepa, sepx, seps] + sw3r + [sw2x] + sacc8
               + [sconv, st, sv, ss, sg, so])
        p.eng_sem = {"vector": sv, "scalar": ss, "gpsimd": sg, "tensor": st}
        swc_prev = [None]
        def CDMA(fn, extra=()):
            t = OP("sync", fn, waits=(swc_prev[0],) + tuple(extra),
                   inc=swc, inck=16)
            swc_prev[0] = t
            return t

        def T(name, shape, dt):
            return stk.enter_context(nc.sbuf_tensor(name, shape, dt))

        io_p = T("io_p", [128, 128], F32)
        io_c = T("io_c", [128, 128], F32)
        id_f = T("id_f", [128, 128], F32)
        id_b = T("id_b", [128, 128], BF16)
        ones_r = T("ones_r", [1, 128], F32)
        ones_cb = T("ones_cb", [128, 1], BF16)
        iota32 = T("iota32", [128, 32], F32)
        pm_t = T("pm_t", [128, 32], F32)
        gs_t = T("gs_t", [128, 16 * 17], F32)
        bq_r = T("bq_r", [128, 48], F32)
        bq_s = T("bq_s", [128, 16], F32)
        bvf = T("bvf", [1, H], F32)
        coef = T("coef", [128, 16 * NE], F32)
        mxk2 = T("mxk2", [1, NH], F32)
        mxq2 = T("mxq2", [1, NH], F32)
        m_h = T("m_h", [1, NH], F32)
        sm = T("smalls", [128, 64], F32)
        block = stk.enter_context(nc.Block())
        OP = p.op

        OP("gpsimd", lambda g: g.iota(io_p[:], pattern=[[0, 128]], base=0,
           channel_multiplier=1, allow_small_or_imprecise_dtypes=True), inc=sg)
        OP("gpsimd", lambda g: g.iota(io_c[:], pattern=[[1, 128]], base=0,
           channel_multiplier=0, allow_small_or_imprecise_dtypes=True), inc=sg)
        OP("gpsimd", lambda g: g.iota(iota32[:], pattern=[[128, 32]], base=0,
           channel_multiplier=1, allow_small_or_imprecise_dtypes=True), inc=sg)
        OP("gpsimd", lambda g: g.memset(ones_r[:], 1.0), inc=sg)
        OP("gpsimd", lambda g: g.memset(ones_cb[:], 1.0), inc=sg)
        OP("gpsimd", lambda g: g.memset(mxk2[:], 0.0), inc=sg)
        tok_ms = OP("gpsimd", lambda g: g.memset(mxq2[:], 0.0), inc=sg)
        tok_idf = OP("vector", lambda v: v.tensor_tensor(id_f[:], io_p[:], io_c[:],
                     ALU.is_equal), waits=((sg, 2),), inc=sv)
        tok_idb = OP("vector", lambda v: v.tensor_copy(id_b[:], id_f[:]), inc=sv)
        tok_pm = CDMA(lambda q: q.dma_start(out=pm_t[:], in_=pm_in))
        tok_cns = CDMA(lambda q: q.dma_start(
            out=gs_t[:].rearrange("p (c n) -> p c n", c=16),
            in_=gs_in.rearrange("(c p) n -> p c n", p=128)))
        tok_bq = CDMA(lambda q: q.dma_start(out=bq_r[:], in_=bqkv_r))
        tok_bvf = CDMA(lambda q: q.dma_start(out=bvf[:],
                       in_=bqkv_f[0:1, 2 * H:3 * H]))
        tok_bqs = OP("vector", lambda v: v.tensor_scalar_mul(bq_s[:], bq_r[:, 0:16],
                     ISQ), waits=(tok_bq,), inc=sv)

        def _phaseA(xb, sqd, stats, xn_b, xn_f, xnT_f, xnT_mg, wkb,
                    kqe, ksq, rt, rtt, ptb, ptf, pa):
            pab = [pa[:, i * 512:(i + 1) * 512] for i in range(6)]
            xnT_2 = xnT_mg[:].rearrange("p (r c t) -> p r c t", r=2, c=16)

            t_norm, t_sq, t_ssq, t_trd = {}, {}, {}, {}
            tok_xnT_dma = {}
            tok_prj_done = {}
            ptb_free = tok_idb
            ptf_free = [tok_idf]
            xnf_free = {}
            rt_prev = None
            wkb_free = [None, None]
            kqe_free = [None, None]
            pa03_free = []
            pa4_free = [None]
            pa5_free = [None]

            def emit_tiles(mg):
                nonlocal ptb_free, ptf_free, rt_prev, pa5_free
                own = mg >= 2
                xnT_v = xnT_2[:, mg % 2]
                for tl in range(8):
                    tt = 8 * mg + tl
                    xs = tt % 3
                    xt = xb[:, xs * H:(xs + 1) * H]
                    w = []
                    if tt >= 3:
                        w = [(sv, t_sq[tt - 3][1]), (ss, t_norm[tt - 3][1])]
                    tok_x = OP("sync", lambda q, tt=tt, xt=xt: q.dma_start(
                        out=xt, in_=x_in[128 * tt:128 * (tt + 1), :]),
                        waits=tuple(w), inc=sxA[tt % 3], inck=16)
                    sb = (tt % 2) * 8
                    s1 = stats[:, sb:sb + 1]; ssqc = stats[:, sb + 1:sb + 2]
                    mean = stats[:, sb + 2:sb + 3]; msq = stats[:, sb + 3:sb + 4]
                    var = stats[:, sb + 4:sb + 5]; lv = stats[:, sb + 5:sb + 6]
                    rstd = stats[:, sb + 6:sb + 7]; nb = stats[:, sb + 7:sb + 8]
                    w2_ = (ss, t_norm[tt - 2][1]) if tt >= 2 else None
                    t_sq[tt] = OP("vector", lambda v, xt=xt, s1=s1: v.reduce_sum(
                        s1, xt, axis=AX.X), waits=(tok_x, w2_), inc=sv)
                    t_ssq[tt] = OP("scalar", lambda s, xt=xt, ssqc=ssqc:
                        s.activation(sqd[:], xt, AF.Square, accum_out=ssqc),
                        waits=(tok_x, w2_), inc=ss)
                    OP("vector", lambda v, s1=s1, mean=mean: v.tensor_scalar_mul(
                        mean, s1, 1.0 / H), inc=sv)
                    OP("vector", lambda v, mean=mean, msq=msq: v.tensor_mul(
                        msq, mean, mean), inc=sv)
                    OP("vector", lambda v, msq=msq: v.tensor_scalar(
                        msq, msq, EPS, None, ALU.subtract), inc=sv)
                    tok_var = OP("vector", lambda v, ssqc=ssqc, msq=msq, var=var:
                        v.scalar_tensor_tensor(var, ssqc, 1.0 / H, msq, ALU.mult,
                        ALU.subtract), waits=(t_ssq[tt],), inc=sv)
                    OP("scalar", lambda s, var=var, lv=lv: s.activation(
                        lv, var, AF.Ln), waits=(tok_var,), inc=ss)
                    tok_rstd = OP("scalar", lambda s, lv=lv, rstd=rstd: s.activation(
                        rstd, lv, AF.Exp, scale=-0.5), inc=ss)
                    tok_nb = OP("vector", lambda v, mean=mean, rstd=rstd, nb=nb:
                        v.scalar_tensor_tensor(nb, mean, -1.0, rstd, ALU.mult,
                        ALU.mult), waits=(tok_rstd,), inc=sv)
                    prev_same = tt - 2 if (tt - 2 >= 0 and (tt - 2 >= 16) == own) \
                        else None
                    wnorm = (st, t_trd[prev_same]) if prev_same is not None else None
                    nsl = tl % 2
                    if own:
                        dstn = xn_f[:, nsl * H:(nsl + 1) * H]
                    else:
                        dstn = xn_b[:, nsl * H:(nsl + 1) * H]
                    t_norm[tt] = OP("scalar", lambda s, xt=xt, o=dstn, rstd=rstd,
                        nb=nb: s.activation(o, xt, AF.Identity, bias=nb, scale=rstd),
                        waits=(tok_nb, wnorm), inc=ss)

                    wmg = []
                    if tl == 0 and mg > 1:
                        wmg.append((st, tok_prj_done[mg - 2]))
                        if mg - 2 >= 2:
                            wmg.append(tok_xnT_dma[mg - 2])
                    if 'tr' in ABLATE:
                        t_trd[tt] = st.n
                        continue
                    if own:
                        src = xn_f[:, nsl * H:(nsl + 1) * H]
                        for b4 in range(4):
                            for c4 in range(4):
                                c = 4 * b4 + c4
                                wtr = [t_norm[tt]]
                                if c4 == 0:
                                    wtr += [tok_idf] + list(ptf_free) + wmg
                                    wmg = []
                                tok_tr = OP("tensor", lambda t, src=src, c=c, c4=c4:
                                    t.transpose(ptf[:, 128 * c4:128 * (c4 + 1)],
                                    src[:, 128 * c:128 * (c + 1)], id_f[:]),
                                    waits=tuple(wtr), inc=st)
                            wev = [tok_tr]
                            if rt_prev is not None:
                                wev.append(rt_prev)
                            tok_evf = OP("vector", lambda v, b4=b4: v.tensor_copy(
                                xnT_f[:, 512 * b4:512 * (b4 + 1)], ptf[:]),
                                waits=tuple(wev), inc=sv)
                            tok_evs = OP("scalar", lambda s, b4=b4, tl=tl,
                                xnT_v=xnT_v: s.activation(
                                xnT_v[:, 4 * b4:4 * b4 + 4, 128 * tl:128 * (tl + 1)],
                                ptf[:].rearrange("p (c f) -> p c f", c=4), AF.Copy),
                                waits=(tok_tr, tok_evf), inc=ss)
                            ptf_free = [tok_evf, tok_evs]
                        t_trd[tt] = st.n
                        for c in (range(16) if 'router' not in ABLATE else []):
                            wrt = []
                            if c == 0:
                                wrt = [tok_evf, tok_cns] + pa5_free
                            OP("tensor", lambda t, c=c: t.matmul(
                                pab[5][:, 0:17], xnT_f[:, 128 * c:128 * (c + 1)],
                                gs_t[:].rearrange("p (c n) -> p c n", c=16)[:, c, :],
                                start=(c == 0), stop=(c == 15)),
                                waits=tuple(wrt), inc=st)
                        rt_prev = (st, st.n) if 'router' not in ABLATE else rt_prev
                        skip_topk = 'router' in ABLATE
                        ttg = tt - 16
                        cslab = coef[:, NE * ttg:NE * (ttg + 1)]
                        m1 = sm[:, 0:1]; negm = sm[:, 1:2]; sumw = sm[:, 2:3]
                        rsum = sm[:, 3:4]; t4 = sm[:, 4:5]
                        if skip_topk:
                            continue
                        tok_rtc = OP("vector", lambda v: v.tensor_copy(rt[:, 0:17],
                           pab[5][:, 0:17]), waits=(rt_prev,), inc=sv)
                        pa5_free = [tok_rtc]
                        OP("vector", lambda v: v.reduce_max(m1, rt[:, 0:16],
                           axis=AX.X), inc=sv)
                        tok_negm = OP("vector", lambda v: v.tensor_scalar_mul(
                            negm, m1, -1.0), inc=sv)
                        tok_th = OP("scalar", lambda s: s.activation(
                            rtt[:, 16:17], rt[:, 16:17], AF.Tanh, scale=0.5),
                            waits=(tok_rtc,), inc=ss)
                        tok_exp = OP("scalar", lambda s: s.activation(
                            rt[:, 17:33], rt[:, 0:16], AF.Exp, bias=negm,
                            accum_out=sumw), waits=(tok_negm,), inc=ss)
                        OP("vector", lambda v: v.tensor_copy(rtt[:, 0:16],
                           rt[:, 0:16]), waits=(tok_exp,), inc=sv)
                        for _ in range(3):
                            OP("vector", lambda v: v.reduce_max(t4, rtt[:, 0:16],
                               axis=AX.X), inc=sv)
                            OP("vector", lambda v: v.tensor_scalar(
                               sm[:, 6:22], rtt[:, 0:16], t4, None,
                               ALU.is_equal), inc=sv)
                            OP("vector", lambda v: v.scalar_tensor_tensor(
                               rtt[:, 0:16], sm[:, 6:22], -1e30, rtt[:, 0:16],
                               ALU.mult, ALU.add), inc=sv)
                        OP("vector", lambda v: v.reduce_max(t4, rtt[:, 0:16],
                           axis=AX.X), inc=sv)
                        OP("vector", lambda v: v.tensor_scalar(
                           sm[:, 6:22], rt[:, 0:16], t4, None, ALU.is_ge), inc=sv)
                        OP("vector", lambda v: v.reciprocal(rsum, sumw), inc=sv)
                        OP("vector", lambda v, cslab=cslab: v.tensor_tensor(
                           cslab[:, 4:20], rt[:, 17:33], sm[:, 6:22], ALU.mult),
                           inc=sv)
                        OP("vector", lambda v, cslab=cslab: v.tensor_scalar_mul(
                           cslab[:, 4:20], cslab[:, 4:20], rsum), inc=sv)
                        for c4 in range(4):
                            OP("vector", lambda v, cslab=cslab, c4=c4:
                               v.tensor_scalar(cslab[:, c4:c4 + 1], rtt[:, 16:17],
                               0.5, 0.5, ALU.mult, ALU.add),
                               waits=(tok_th,) if c4 == 0 else (), inc=sv)
                    else:
                        src = xn_b[:, nsl * H:(nsl + 1) * H]
                        for b8 in range(2):
                            for c8 in range(8):
                                c = 8 * b8 + c8
                                wtr = [t_norm[tt]]
                                if c8 == 0:
                                    wtr += [tok_idb, ptb_free] + wmg
                                    wmg = []
                                tok_tr = OP("tensor", lambda t, src=src, c=c, c8=c8:
                                    t.transpose(ptb[:, 128 * c8:128 * (c8 + 1)],
                                    src[:, 128 * c:128 * (c + 1)], id_b[:]),
                                    waits=tuple(wtr), inc=st)
                            tok_evb = OP("vector", lambda v, b8=b8, tl=tl,
                                xnT_v=xnT_v: v.tensor_copy(
                                xnT_v[:, 8 * b8:8 * b8 + 8, 128 * tl:128 * (tl + 1)],
                                ptb[:].rearrange("p (c f) -> p c f", c=8)),
                                waits=(tok_tr,), inc=sv)
                            ptb_free = tok_evb
                        t_trd[tt] = st.n

                tok_xnT_v = (sv, sv.n)
                tok_xnT_s = (ss, ss.n)
                return (sv, sv.n), (ss, ss.n)

            def emit_proj(mg, tok_xnT_v, tok_xnT_s):
                nonlocal pa03_free, pa4_free
                own = mg >= 2
                xnT_v = xnT_2[:, mg % 2]
                for proj in ((["k", "q", "v"] if own else ["k", "v"])
                             if 'kq' not in ABLATE else []):
                    dst = {"k": k_s, "q": q_s, "v": v_s}[proj]
                    dcol = 1024 * (mg - 2) if proj == "q" else 1024 * mg
                    mx = mxk2 if proj == "k" else mxq2
                    for fc in range(16):
                        slot = fc % 2
                        qk_idx = {"q": fc, "k": 16 + fc, "v": 32 + fc}[proj]
                        tok_w = OP("gpsimd", lambda g, slot=slot, qk_idx=qk_idx:
                            g.dma_start(
                            out=wkb[:, slot * H:(slot + 1) * H]
                            .rearrange("p (kc f) -> p kc f", kc=16),
                            in_=wqk[qk_idx]),
                            waits=(wkb_free[slot],), inc=swk2[slot], inck=16)
                        for k in range(16):
                            wm = []
                            if k == 0:
                                wm = [tok_w, tok_xnT_v, tok_xnT_s]
                                if fc in (0, 1):
                                    wm += pa03_free
                                else:
                                    wm.append(kqe_evtok[slot])
                            for tk in range(2):
                                OP("tensor", lambda t, slot=slot, k=k, tk=tk,
                                    xnT_v=xnT_v: t.matmul(pab[2 * slot + tk],
                                    wkb[:, slot * H + 128 * k:
                                         slot * H + 128 * (k + 1)],
                                    xnT_v[:, k, 512 * tk:512 * (tk + 1)],
                                    start=(k == 0), stop=(k == 15)),
                                    waits=tuple(wm) if tk == 0 else (), inc=st)
                        tok_mm = (st, st.n)
                        wkb_free[slot] = tok_mm
                        if proj == "q":
                            bias_ap, scale = bq_s[:, fc:fc + 1], ISQ
                        elif proj == "k":
                            bias_ap, scale = bq_r[:, 16 + fc:17 + fc], 1.0
                        else:
                            bias_ap, scale = bq_r[:, 32 + fc:33 + fc], 1.0
                        tok_ev = OP("scalar", lambda s, slot=slot, bias_ap=bias_ap,
                            scale=scale: s.activation(
                            kqe[:, slot * 1024:(slot + 1) * 1024],
                            pa[:, 1024 * slot:1024 * (slot + 1)], AF.Identity,
                            bias=bias_ap, scale=scale),
                            waits=(tok_mm, kqe_free[slot], tok_bqs), inc=ss)
                        if fc == 0:
                            kqe_evtok = {}
                        kqe_evtok[slot] = tok_ev
                        if proj != "v":
                            tok_ksq = OP("scalar", lambda s, slot=slot:
                                s.activation(
                                ksq[:], kqe[:, slot * 1024:(slot + 1) * 1024],
                                AF.Square), waits=(tok_ev,), inc=ss)
                            for tk in range(2):
                                OP("tensor", lambda t, tk=tk: t.matmul(
                                    pab[4][0:1, :], ones_cb[:],
                                    ksq[:, 512 * tk:512 * (tk + 1)],
                                    start=(tk == 0), stop=(tk == 1)),
                                    waits=(tok_ksq, (sg, 5), pa4_free[0])
                                    if tk == 0 else (), inc=st)
                            tok_nmm = (st, st.n)
                            OP("vector", lambda v: v.reduce_max(sm[0:1, 8:9],
                               pab[4][0:1, :], axis=AX.X),
                               waits=(tok_nmm, (sg, tok_ms[1])), inc=sv)
                            pa4_free = [(sv, sv.n)]
                            OP("vector", lambda v, mx=mx, fc=fc: v.tensor_tensor(
                               mx[0:1, fc:fc + 1], mx[0:1, fc:fc + 1],
                               sm[0:1, 8:9], ALU.max), inc=sv)
                        kqe_free[slot] = OP("sync", lambda q, dst=dst, fc=fc,
                            dcol=dcol, slot=slot: q.dma_start(
                            out=dst[128 * fc:128 * (fc + 1), dcol:dcol + 1024],
                            in_=kqe[:, slot * 1024:(slot + 1) * 1024]),
                            waits=(tok_ev,), inc=skqr[slot], inck=16)
                    pa03_free = [kqe_evtok[0], kqe_evtok[1]]

                tok_prj_done[mg] = st.n

                if own:
                    tok_xnT_dma[mg] = OP("sync", lambda q, mg=mg, xnT_v=xnT_v:
                        q.dma_start(
                        out=xnT_s.rearrange("(c p) t -> p c t", p=128)
                        [:, :, 1024 * (mg - 2):1024 * (mg - 1)],
                        in_=xnT_v[:, :, :]),
                        waits=(tok_xnT_v, tok_xnT_s,
                               tok_xnT_dma.get(mg - 1)), inc=sxn, inck=16)


            ttoks = {}
            ttoks[0] = emit_tiles(0)
            ttoks[1] = emit_tiles(1)
            emit_proj(0, *ttoks[0])
            ttoks[2] = emit_tiles(2)
            emit_proj(1, *ttoks[1])
            ttoks[3] = emit_tiles(3)
            emit_proj(2, *ttoks[2])
            emit_proj(3, *ttoks[3])

            if 'mh' in ABLATE:
                return [(s_, s_.n) for s_ in ALL if s_.n > 0], None
            OP("scalar", lambda s: s.activation(sm[0:1, 16:32], mxk2[:], AF.Ln),
               waits=((sv, sv.n),), inc=ss)
            tok_lnq = OP("scalar", lambda s: s.activation(sm[0:1, 32:48], mxq2[:],
                         AF.Ln), inc=ss)
            tok_mhv = OP("vector", lambda v: v.tensor_add(sm[0:1, 48:64],
                         sm[0:1, 16:32], sm[0:1, 32:48]), waits=(tok_lnq,), inc=sv)
            tok_mh = OP("scalar", lambda s: s.activation(m_h[:], sm[0:1, 48:64],
                        AF.Exp, scale=0.5), waits=(tok_mhv,), inc=ss)

            if dbgA:
                CDMA(lambda q: q.dma_start(out=dbg_coef, in_=coef[:]),
                     extra=((sv, sv.n),))
                OP("vector", lambda v: v.tensor_copy(sm[0:1, 16:32], mxq2[:]),
                   waits=(tok_mh,), inc=sv)
                OP("vector", lambda v: v.tensor_copy(sm[0:1, 32:48], mxk2[:]),
                   inc=sv)
                tok_dm = OP("vector", lambda v: v.tensor_copy(sm[0:1, 48:64],
                            m_h[:]), inc=sv)
                CDMA(lambda q: q.dma_start(out=dbg_m, in_=sm[0:1, 16:64]),
                     extra=(tok_dm,))

            return [(s_, s_.n) for s_ in ALL if s_.n > 0], tok_mh

        with contextlib.ExitStack() as stkA:
            xb = stkA.enter_context(nc.sbuf_tensor("xb", [128, 3 * H], F32))
            sqd = stkA.enter_context(nc.sbuf_tensor("sqd", [128, H], BF16))
            stats = stkA.enter_context(nc.sbuf_tensor("stats", [128, 16], F32))
            xn_b = stkA.enter_context(nc.sbuf_tensor("xn_b", [128, 2 * H], BF16))
            xn_f = stkA.enter_context(nc.sbuf_tensor("xn_f", [128, 2 * H], F32))
            xnT_f = stkA.enter_context(nc.sbuf_tensor("xnT_f", [128, H], F32))
            xnT_mg = stkA.enter_context(nc.sbuf_tensor("xnT_mg", [128, 2 * 16 * 1024], BF16))
            wkb = stkA.enter_context(nc.sbuf_tensor("wkb", [128, 2 * H], BF16))
            kqe = stkA.enter_context(nc.sbuf_tensor("kqe", [128, 2 * 1024], BF16))
            ksq = stkA.enter_context(nc.sbuf_tensor("ksq", [128, 1024], BF16))
            rt = stkA.enter_context(nc.sbuf_tensor("rt", [128, 40], F32))
            rtt = stkA.enter_context(nc.sbuf_tensor("rtt", [128, 24], F32))
            ptb = stkA.enter_context(nc.psum_tensor("ptb", [128, 1024], BF16))
            ptf = stkA.enter_context(nc.psum_tensor("ptf", [128, 512], F32))
            pa = stkA.enter_context(nc.psum_tensor("pa", [128, 6 * 512], F32))
            A_end, tok_mh = _phaseA(xb, sqd, stats, xn_b, xn_f, xnT_f, xnT_mg,
                                    wkb, kqe, ksq, rt, rtt, ptb, ptf, pa)

        if dbgA:
            for eng in ENGS:
                for w in A_end:
                    p.wait(eng, *w)
            p.emit(block)
            return nc

        def _phaseB(kb, vb, qb, pT, sdg, biasb, pmh, augr, rlx, bcs, afm,
                    wo_sb, bob, s_e, pb, pl, pm1, vt, pvt, iota_q, d4_t, m4_t,
                    bof, mhb, tok_mh):
            conv_tok = {e: None for e in range(NE)}
            pbb = [pb[:, i * 512:(i + 1) * 512] for i in range(4)]
            for eng in ENGS:
                for w in A_end:
                    p.wait(eng, *w)

            tok_wo = OP("gpsimd", lambda g: g.dma_start(
                out=wo_sb[:].rearrange("p (c f) -> p c f", c=16),
                in_=wo_in.rearrange("(c p) f -> p c f", p=128)), inc=swo, inck=16)
            OP("gpsimd", lambda g: g.iota(iota_q[:], pattern=[[1, 2048]],
               base=OWN0, channel_multiplier=0,
               allow_small_or_imprecise_dtypes=True), inc=sg)
            tok_iq = (sg, sg.n)
            CDMA(lambda q: q.dma_start(out=d4_t[:], in_=d4_in))
            tok_d4 = CDMA(lambda q: q.dma_start(out=m4_t[:], in_=m4_in))
            tok_bof = CDMA(lambda q: q.dma_start(out=bof[:], in_=bo_f))
            for v4 in range(4):
                OP("tensor", lambda t, v4=v4: t.matmul(
                    pbb[v4], ones_r[:], bof[0:1, 512 * v4:512 * (v4 + 1)],
                    start=True, stop=True), waits=(tok_bof,) if v4 == 0 else (),
                    inc=st)
            tok_bob = OP("vector", lambda v: v.tensor_copy(bob[:], pb[:, 0:2048]),
                         waits=((st, st.n),), inc=sv)
            tok_mb = OP("tensor", lambda t: t.matmul(pm1[:, 0:16], ones_r[:],
                m_h[0:1, 0:16], start=True, stop=True),
                waits=(tok_mh, tok_bob), inc=st)
            tok_mhb = OP("vector", lambda v: v.tensor_copy(mhb[:], pm1[:, 0:16]),
                waits=(tok_mb,), inc=sv)

            kb_free = [None, None]; vb_free = [None, None]; qb_free = [None, None]
            pT_free = [None] * 4
            vt_free = [None] * 4
            pvt_free = [None] * 2
            pbS_free = [tok_bob, tok_mhb]
            pb2_free = tok_bob
            pb3_free = tok_bob
            pl_free = None
            pmh_rd = None
            aug_rd = [None, None]
            afm_rd = None
            s_e_free = [None, None]
            sdg_rd = [None, None]

            for qt in range(4):
                qend = OWN0 + 512 * (qt + 1)
                for h in range(NH):
                    js, nj = _jr(qt, h)
                    slot = h % 2
                    colf = h >= COLF
                    OP("sync", lambda q, slot=slot, h=h, js=js, qend=qend:
                        q.dma_start(out=kb[:, slot * SEQ:slot * SEQ + (qend - js)],
                        in_=k_s[128 * h:128 * (h + 1), js:qend]),
                        waits=(kb_free[slot],), inc=skvr[slot], inck=16)
                    OP("sync", lambda q, slot=slot, h=h, js=js, qend=qend:
                        q.dma_start(out=vb[:, slot * SEQ:slot * SEQ + (qend - js)],
                        in_=v_s[128 * h:128 * (h + 1), js:qend]),
                        waits=(vb_free[slot],), inc=skvr[slot], inck=16)
                    tok_qb = OP("sync", lambda q, slot=slot, h=h, qt=qt: q.dma_start(
                        out=qb[:, slot * 512:(slot + 1) * 512],
                        in_=q_s[128 * h:128 * (h + 1), 512 * qt:512 * (qt + 1)]),
                        waits=(qb_free[slot],), inc=skvr[slot], inck=16)
                    tok_kb = tok_qb
                    tok_vb = tok_qb
                    if not colf:
                        tok_aug = OP("vector", lambda v, slot=slot, h=h, qt=qt:
                            v.tensor_scalar_mul(
                            augr[0:1, slot * 512:(slot + 1) * 512],
                            iota_q[0:1, 512 * qt:512 * (qt + 1)], -SLOPES[h]),
                            waits=(aug_rd[slot], tok_iq), inc=sv)
                    off = -SLOPES[h] * (qend - 1) if colf else 0.0
                    tok_negm2 = OP("vector", lambda v, off=off, h=h: v.tensor_scalar(
                        sm[:, 9:10], mhb[:, h:h + 1], -1.0, off, ALU.mult, ALU.add),
                        waits=(tok_mhb,), inc=sv)
                    tok_pmh = OP("vector", lambda v: v.tensor_scalar(
                        pmh[:], pm_t[:], sm[:, 9:10], None, ALU.add),
                        waits=(pmh_rd,), inc=sv)

                    expS = {}
                    vte_toks = {}
                    Sbanks = [pbb[0], pbb[1], pm1[:]]

                    def emit_av(j):
                        pslot_ = j % 4
                        vslot_ = j % 4
                        wA = [expS[j], vte_toks[j]]
                        if j == 0:
                            wA += [pb2_free]
                        OP("tensor", lambda t, vslot=vslot_, j=j, pslot=pslot_,
                            nj=nj: t.matmul(pbb[2],
                            vt[:, 128 * vslot:128 * (vslot + 1)],
                            pT[:, 512 * pslot:512 * (pslot + 1)],
                            start=(j == 0), stop=(j == nj - 1)),
                            waits=tuple(wA), inc=st)
                        pT_free[pslot_] = OP("tensor", lambda t, pslot=pslot_,
                            j=j, nj=nj: t.matmul(pl[:], ones_cb[:],
                            pT[:, 512 * pslot:512 * (pslot + 1)],
                            start=(j == 0), stop=(j == nj - 1)),
                            waits=(pl_free,) if j == 0 else (), inc=st)
                        vt_free[vslot_] = pT_free[pslot_]

                    for jt in range(nj):
                        JT = js // 128 + jt
                        diag = jt >= nj - 4
                        sbk = Sbanks[jt % 3]
                        wS = []
                        if jt < 3:
                            wS += pbS_free
                            wS += [tok_kb, tok_qb]
                        else:
                            wS.append(expS[jt - 3])
                        if (not colf) and (not diag):
                            OP("tensor", lambda t, sbk=sbk, slot=slot: t.matmul(
                                sbk, ones_r[:],
                                augr[0:1, slot * 512:(slot + 1) * 512],
                                start=True, stop=False),
                                waits=tuple(wS) + (tok_aug,))
                            tok_S = OP("tensor", lambda t, sbk=sbk, slot=slot,
                                jt=jt: t.matmul(sbk,
                                kb[:, slot * SEQ + 128 * jt:
                                    slot * SEQ + 128 * (jt + 1)],
                                qb[:, slot * 512:(slot + 1) * 512],
                                start=False, stop=True), inc=st)
                        else:
                            tok_S = OP("tensor", lambda t, sbk=sbk, slot=slot,
                                jt=jt: t.matmul(sbk,
                                kb[:, slot * SEQ + 128 * jt:
                                    slot * SEQ + 128 * (jt + 1)],
                                qb[:, slot * 512:(slot + 1) * 512],
                                start=True, stop=True), waits=tuple(wS), inc=st)
                        bslot = jt % 4
                        pslot = jt % 4
                        if colf or (not diag):
                            tok_bias = OP("vector", lambda v, bslot=bslot, JT=JT,
                                h=h: v.scalar_tensor_tensor(
                                biasb[:, bslot:bslot + 1], iota32[:, JT:JT + 1],
                                SLOPES[h], pmh[:, JT:JT + 1], ALU.mult, ALU.add),
                                waits=(tok_pmh,
                                       expS.get(jt - 4)), inc=sv)
                            bias_ap = biasb[:, bslot:bslot + 1]
                            bw = tok_bias
                        else:
                            bias_ap = pmh[:, JT:JT + 1]
                            bw = tok_pmh
                        if diag:
                            m4i = jt - (nj - 4)
                            ds_ = jt % 2
                            dt_ = m4_t if colf else d4_t
                            sc_ = 1.0 if colf else SLOPES[h]
                            tok_dg = OP("vector", lambda v, m4i=m4i, sbk=sbk,
                                dt_=dt_, sc_=sc_, ds_=ds_:
                                v.scalar_tensor_tensor(
                                sdg[:, 512 * ds_:512 * (ds_ + 1)],
                                dt_[:, 512 * m4i:512 * (m4i + 1)], sc_,
                                sbk, ALU.mult, ALU.add),
                                waits=(tok_S, sdg_rd[ds_], tok_d4), inc=sv)
                            expS[jt] = OP("scalar", lambda s, pslot=pslot,
                                bias_ap=bias_ap, ds_=ds_: s.activation(
                                pT[:, 512 * pslot:512 * (pslot + 1)],
                                sdg[:, 512 * ds_:512 * (ds_ + 1)],
                                AF.Exp, bias=bias_ap),
                                waits=(tok_dg, bw, pT_free[pslot]), inc=ss)
                            sdg_rd[ds_] = expS[jt]
                        else:
                            expS[jt] = OP("scalar", lambda s, pslot=pslot, sbk=sbk,
                                bias_ap=bias_ap: s.activation(
                                pT[:, 512 * pslot:512 * (pslot + 1)], sbk,
                                AF.Exp, bias=bias_ap),
                                waits=(tok_S, bw, pT_free[pslot]), inc=ss)
                        vslot = jt % 4
                        vbk = jt % 2
                        tok_vt = OP("tensor", lambda t, slot=slot, jt=jt,
                            vbk=vbk: t.transpose(
                            pvt[:, 1024 * vbk:1024 * vbk + 128],
                            vb[:, slot * SEQ + 128 * jt:
                                slot * SEQ + 128 * (jt + 1)], id_b[:]),
                            waits=(tok_vb, tok_idb, pvt_free[vbk]), inc=st)
                        vte_toks[jt] = OP("vector", lambda v, vslot=vslot,
                            vbk=vbk:
                            v.tensor_copy(vt[:, 128 * vslot:128 * (vslot + 1)],
                            pvt[:, 1024 * vbk:1024 * vbk + 128]),
                            waits=(tok_vt, vt_free[vslot]), inc=sv)
                        pvt_free[vbk] = vte_toks[jt]
                        if jt >= 2:
                            emit_av(jt - 2)
                    emit_av(nj - 2)
                    emit_av(nj - 1)
                    tok_sweep = (st, st.n)
                    kb_free[slot] = tok_sweep
                    vb_free[slot] = tok_sweep
                    qb_free[slot] = tok_sweep
                    aug_rd[slot] = tok_sweep
                    pmh_rd = (ss, ss.n)
                    tok_rl = OP("vector", lambda v: v.reciprocal(rlx[:], pl[:]),
                                waits=(tok_sweep,), inc=sv)
                    pl_free = tok_rl
                    tok_bc = OP("tensor", lambda t: t.matmul(pbb[3], ones_r[:],
                        rlx[:], start=True, stop=True),
                        waits=(tok_rl, pb3_free), inc=st)
                    tok_bcs = OP("vector", lambda v: v.tensor_copy(bcs[:], pbb[3]),
                        waits=(tok_bc,), inc=sv)
                    pb3_free = tok_bcs
                    tok_at = OP("vector", lambda v, h=h: v.tensor_tensor(
                        afm[:, 512 * h:512 * (h + 1)], pbb[2], bcs[:], ALU.mult),
                        waits=(tok_bcs, afm_rd), inc=sv)
                    pb2_free = tok_at
                    pbS_free = [(ss, ss.n)]
                    afm_done = tok_at
                tok_se = None
                for tl4 in range(4):
                    for c in range(16):
                        wW = []
                        if c == 0:
                            wW = [afm_done, tok_wo, pb2_free, pb3_free,
                                  s_e_free[tl4 % 2]]
                            wW += pbS_free
                            if tok_se is not None:
                                wW.append(tok_se)
                        for f4 in range(4):
                            OP("tensor", lambda t, c=c, tl4=tl4, f4=f4: t.matmul(
                                pbb[f4],
                                afm[:, 512 * c + 128 * tl4:
                                    512 * c + 128 * (tl4 + 1)],
                                wo_sb[:, H * c + 512 * f4:H * c + 512 * (f4 + 1)],
                                start=(c == 0), stop=(c == 15)),
                                waits=tuple(wW) if f4 == 0 else (), inc=st)
                    tok_womm = (st, st.n)
                    es = tl4 % 2
                    tok_se = OP("vector", lambda v, es=es: v.tensor_tensor(
                        s_e[:, es * H:(es + 1) * H], pb[:, 0:2048], bob[:],
                        ALU.add), waits=(tok_womm, tok_bob), inc=sv)
                    pbS_free = [tok_se]
                    pb2_free = tok_se
                    pb3_free = tok_se
                    s_e_free[es] = OP("sync", lambda q, qt=qt, tl4=tl4, es=es:
                        q.dma_start(out=s_s[512 * qt + 128 * tl4:
                        512 * qt + 128 * (tl4 + 1), :],
                        in_=s_e[:, es * H:(es + 1) * H]),
                        waits=(tok_se,), inc=sser[es], inck=16)
                afm_rd = (st, st.n)

            return [(s_, s_.n) for s_ in ALL if s_.n > 0], conv_tok

        with contextlib.ExitStack() as stkB:
            kb = stkB.enter_context(nc.sbuf_tensor("kb", [128, 2 * SEQ], BF16))
            vb = stkB.enter_context(nc.sbuf_tensor("vb", [128, 2 * SEQ], BF16))
            qb = stkB.enter_context(nc.sbuf_tensor("qb", [128, 2 * 512], BF16))
            pT = stkB.enter_context(nc.sbuf_tensor("pT", [128, 4 * 512], BF16))
            sdg = stkB.enter_context(nc.sbuf_tensor("sdg", [128, 1024], F32))
            biasb = stkB.enter_context(nc.sbuf_tensor("biasb", [128, 8], F32))
            pmh = stkB.enter_context(nc.sbuf_tensor("pmh", [128, 32], F32))
            augr = stkB.enter_context(nc.sbuf_tensor("augr", [1, 2 * 512], F32))
            rlx = stkB.enter_context(nc.sbuf_tensor("rlx", [1, 512], F32))
            bcs = stkB.enter_context(nc.sbuf_tensor("bcs", [128, 512], F32))
            afm = stkB.enter_context(nc.sbuf_tensor("afm", [128, 16 * 512], BF16))
            wo_sb = stkB.enter_context(nc.sbuf_tensor("wo_sb", [128, 16 * H], BF16))
            bob = stkB.enter_context(nc.sbuf_tensor("bob", [128, H], F32))
            s_e = stkB.enter_context(nc.sbuf_tensor("s_e", [128, 2 * H], F32))
            pb = stkB.enter_context(nc.psum_tensor("pb", [128, 4 * 512], F32))
            pl = stkB.enter_context(nc.psum_tensor("pl", [1, 512], F32))
            pm1 = stkB.enter_context(nc.psum_tensor("pm1", [128, 512], F32))
            vt = stkB.enter_context(nc.sbuf_tensor("vt", [128, 4 * 128], BF16))
            pvt = stkB.enter_context(nc.psum_tensor("pvt", [128, 2048], BF16))
            iota_q = stkB.enter_context(nc.sbuf_tensor("iota_q", [1, 2048], F32))
            d4_t = stkB.enter_context(nc.sbuf_tensor("d4_t", [128, 2048], F32))
            m4_t = stkB.enter_context(nc.sbuf_tensor("m4_t", [128, 2048], F32))
            bof = stkB.enter_context(nc.sbuf_tensor("bof", [1, H], F32))
            mhb = stkB.enter_context(nc.sbuf_tensor("mhb", [128, 16], F32))
            B_end, conv_tok = _phaseB(kb, vb, qb, pT, sdg, biasb, pmh, augr,
                                      rlx, bcs, afm, wo_sb, bob, s_e, pb, pl,
                                      pm1, vt, pvt, iota_q, d4_t, m4_t, bof,
                                      mhb, tok_mh)

        if dbgB:
            for eng in ENGS:
                for w in B_end:
                    p.wait(eng, *w)
            p.emit(block)
            return nc

        def _phaseC(xnh, w13b, w2b, hbuf, accst, silu_t, xe, se2, go, ot, oe, pc,
                    conv_tok):
            for eng in ENGS:
                for w in B_end:
                    p.wait(eng, *w)
            xnh_v = xnh[:].rearrange("p (c t) -> p c t", c=16)

            w13_free = [None, None]
            w2_rd = None
            xnh_rd = None
            pcg_free = [None, None]
            pcu_free = [None, None]
            pw2_free = [None, None]
            acc_dma = [None] * 8
            accst_free = [None] * 4
            pending_accum = []
            xe_free = [None, None]; se_free = [None, None]; oe_free = [None, None]
            ep_go_rd = None

            def emit_xn(hf):
                return OP("sync", lambda q, hf=hf: q.dma_start(
                    out=xnh_v[:, :, :],
                    in_=xnT_s.rearrange("(c p) t -> p c t", p=128)
                    [:, :, 1024 * hf:1024 * (hf + 1)]),
                    waits=(xnh_rd,), inc=sxch, inck=16)
            for hf in range(2):
                tok_xn = emit_xn(hf)
                for e in range(NE):
                    def emit_pair_dma(e, pi):
                        ps_ = pi % 2
                        OP("gpsimd", lambda g, e=e, pi=pi, ps_=ps_:
                            g.dma_start(
                            out=w13b[:, ps_ * 2 * H:ps_ * 2 * H + H]
                            .rearrange("p (k f) -> p k f", k=16),
                            in_=w13a[e, pi]),
                            waits=(w13_free[ps_], conv_tok[e]),
                            inc=sw3r[ps_], inck=16)
                        return OP("gpsimd", lambda g, e=e, pi=pi, ps_=ps_:
                            g.dma_start(
                            out=w13b[:, ps_ * 2 * H + H:(ps_ + 1) * 2 * H]
                            .rearrange("p (k f) -> p k f", k=16),
                            in_=w13a[e, pi + 11]),
                            inc=sw3r[ps_], inck=16)
                    tok_pair_dma = {0: emit_pair_dma(e, 0), 1: emit_pair_dma(e, 1)}

                    def emit_w2_dma(e, i):
                        return OP("gpsimd", lambda g, e=e, i=i: g.dma_start(
                            out=w2b[:, H * i:H * (i + 1)],
                            in_=w2a[e].rearrange("(ic p) f -> p ic f", p=128)
                            [:, i, :]),
                            waits=((w2_rd, conv_tok[e]) if i == 0 else ()),
                            inc=sw2x, inck=16)
                    tok_w2 = None
                    hb_last = None
                    for pi in range(11):
                        ps_ = pi % 2
                        tok_w3b = tok_pair_dma[pi]
                        tok_g = None
                        for piece in range(2):
                            base = ps_ * 2 * H + piece * H
                            pcof = (ps_ * 4 + piece * 2) * 512
                            wfree = pcg_free if piece == 0 else pcu_free
                            for k in range(16):
                                wm = []
                                if k == 0:
                                    wm = [tok_w3b, tok_xn, wfree[ps_]]
                                    if pi < 2:
                                        wm.append(pw2_free[ps_])
                                for tk in range(2):
                                    OP("tensor", lambda t, base=base, k=k, tk=tk,
                                        pcof=pcof: t.matmul(
                                        pc[:, pcof + 512 * tk:
                                           pcof + 512 * (tk + 1)],
                                        w13b[:, base + 128 * k:
                                             base + 128 * (k + 1)],
                                        xnh_v[:, k, 512 * tk:512 * (tk + 1)],
                                        start=(k == 0), stop=(k == 15)),
                                        waits=tuple(wm) if tk == 0 else (),
                                        inc=st)
                            if piece == 0:
                                tok_g = (st, st.n)
                        tok_pair = (st, st.n)
                        w13_free[ps_] = tok_pair
                        if pi + 2 <= 10:
                            tok_pair_dma[pi + 2] = emit_pair_dma(e, pi + 2)
                        if pi <= 8:
                            tok_w2 = emit_w2_dma(e, pi)
                            if pi >= 3 and pending_accum:
                                pending_accum.pop(0)()
                        gof = (ps_ * 4) * 512
                        uof = (ps_ * 4 + 2) * 512
                        tok_si = OP("scalar", lambda s, gof=gof: s.activation(
                            silu_t[:], pc[:, gof:gof + 1024], AF.Silu),
                            waits=(tok_g,), inc=ss)
                        pcg_free[ps_] = tok_si
                        tok_hb = OP("vector", lambda v, pi=pi, uof=uof:
                            v.tensor_tensor(hbuf[:, 1024 * pi:1024 * (pi + 1)],
                            silu_t[:], pc[:, uof:uof + 1024], ALU.mult),
                            waits=(tok_si, tok_pair), inc=sv)
                        pcu_free[ps_] = tok_hb
                        hb_last = tok_hb
                    tok_w2 = emit_w2_dma(e, 9)
                    tok_w2 = emit_w2_dma(e, 10)
                    for th in pending_accum:
                        th()
                    pending_accum = []
                    xnh_rd = (st, st.n)
                    for tl8 in range(8):
                        ttg = 8 * hf + tl8
                        ws_ = tl8 % 2
                        pof = ws_ * 4 * 512
                        for i in range(11):
                            wm = []
                            if i == 0:
                                wm = [tok_w2, hb_last, pw2_free[ws_]]
                            for f4 in range(4):
                                OP("tensor", lambda t, i=i, tl8=tl8, f4=f4,
                                    pof=pof: t.matmul(
                                    pc[:, pof + 512 * f4:pof + 512 * (f4 + 1)],
                                    hbuf[:, 1024 * i + 128 * tl8:
                                         1024 * i + 128 * (tl8 + 1)],
                                    w2b[:, H * i + 512 * f4:
                                        H * i + 512 * (f4 + 1)],
                                    start=(i == 0), stop=(i == 10)),
                                    waits=tuple(wm) if f4 == 0 else (), inc=st)
                        tok_w2mm = (st, st.n)
                        cc = coef[:, NE * ttg + e:NE * ttg + e + 1]
                        sl4 = tl8 % 4
                        tok_stg = OP("vector", lambda v, sl4=sl4, cc=cc, pof=pof:
                            v.tensor_scalar(
                            accst[:, sl4 * H:(sl4 + 1) * H],
                            pc[:, pof:pof + 2048], cc, None, ALU.mult),
                            waits=(tok_w2mm, accst_free[sl4]), inc=sv)
                        pw2_free[ws_] = tok_stg
                        def emit_accum(hf=hf, tl8=tl8, sl4=sl4, e=e,
                                       tok_stg=tok_stg):
                            prev = acc_dma[tl8]
                            if e == 0:
                                t_ = OP("gpsimd", lambda g: g.dma_start(
                                    out=acc_s[1024 * hf + 128 * tl8:
                                              1024 * hf + 128 * (tl8 + 1), :],
                                    in_=accst[:, sl4 * H:(sl4 + 1) * H]),
                                    waits=(tok_stg, prev), inc=sacc8[tl8],
                                    inck=16)
                            else:
                                t_ = OP("gpsimd", lambda g: g.dma_start(
                                    out=acc_s[1024 * hf + 128 * tl8:
                                              1024 * hf + 128 * (tl8 + 1), :],
                                    in_=accst[:, sl4 * H:(sl4 + 1) * H],
                                    accum_op=ALU.add),
                                    waits=(tok_stg, prev), inc=sacc8[tl8],
                                    inck=16)
                            acc_dma[tl8] = t_
                            accst_free[sl4] = t_
                        if tl8 < 4:
                            emit_accum()
                        else:
                            pending_accum.append(emit_accum)
                    w2_rd = (st, st.n)
                for th in pending_accum:
                    th()
                pending_accum = []
                for tl8 in range(8):
                    row = 1024 * hf + 128 * tl8
                    es = 0
                    tok_al = OP("sync", lambda q, row=row, es=es: q.dma_start(
                        out=accst[:, es * H:(es + 1) * H],
                        in_=acc_s[row:row + 128, :]),
                        waits=(acc_dma[tl8], accst_free[es]), inc=sepa, inck=16)
                    tok_x2 = OP("sync", lambda q, row=row, es=es: q.dma_start(
                        out=xe[:, es * H:(es + 1) * H],
                        in_=x_in[OWN0 + row:OWN0 + row + 128, :]),
                        waits=(xe_free[es],), inc=sepx, inck=16)
                    tok_s2 = OP("sync", lambda q, row=row, es=es: q.dma_start(
                        out=se2[:, es * H:(es + 1) * H],
                        in_=s_s[row:row + 128, :]),
                        waits=(se_free[es],), inc=seps, inck=16)
                    tok_gt = OP("scalar", lambda s, es=es: s.activation(
                        go[:], accst[:, es * H:(es + 1) * H], AF.Silu),
                        waits=(tok_al, ep_go_rd), inc=ss)
                    accst_free[es] = tok_gt
                    tok_sg = OP("vector", lambda v, es=es: v.tensor_tensor(
                        ot[:], go[:], se2[:, es * H:(es + 1) * H], ALU.mult),
                        waits=(tok_gt, tok_s2), inc=sv)
                    se_free[es] = tok_sg
                    tok_o = OP("vector", lambda v, es=es: v.tensor_tensor(
                        oe[:, es * H:(es + 1) * H], ot[:],
                        xe[:, es * H:(es + 1) * H], ALU.add),
                        waits=(tok_x2, oe_free[es]), inc=sv)
                    xe_free[es] = tok_o
                    ep_go_rd = tok_sg
                    oe_free[es] = OP("sync", lambda q, row=row, es=es: q.dma_start(
                        out=out_t[row:row + 128, :],
                        in_=oe[:, es * H:(es + 1) * H]),
                        waits=(tok_o,), inc=so, inck=16)

            p.wait("sync", so, so.n)

        with contextlib.ExitStack() as stkC:
            xnh = stkC.enter_context(nc.sbuf_tensor("xnh", [128, 16 * 1024], BF16))
            w13b = stkC.enter_context(nc.sbuf_tensor("w13b", [128, 4 * H], BF16))
            w2b = stkC.enter_context(nc.sbuf_tensor("w2b", [128, 11 * H], BF16))
            hbuf = stkC.enter_context(nc.sbuf_tensor("hbuf", [128, 11 * 1024], BF16))
            accst = stkC.enter_context(nc.sbuf_tensor("accst", [128, 4 * H], F32))
            silu_t = stkC.enter_context(nc.sbuf_tensor("silu_t", [128, 1024], F32))
            xe = stkC.enter_context(nc.sbuf_tensor("xe", [128, H], F32))
            se2 = stkC.enter_context(nc.sbuf_tensor("se2", [128, H], F32))
            go = stkC.enter_context(nc.sbuf_tensor("go", [128, H], F32))
            ot = stkC.enter_context(nc.sbuf_tensor("ot", [128, H], F32))
            oe = stkC.enter_context(nc.sbuf_tensor("oe", [128, H], F32))
            pc = stkC.enter_context(nc.psum_tensor("pc", [128, 8 * 512], F32))
            _phaseC(xnh, w13b, w2b, hbuf, accst, silu_t, xe, se2, go, ot, oe, pc,
                    conv_tok)
            p.emit(block)
    return nc


def _prep_shared(Wqkv, bqkv, Wo, bo, gate_w, sgate_w, w13, w2, sgu, sdn):
    f32 = np.float32
    wqkv = np.ascontiguousarray(np.asarray(Wqkv, f32))
    bq = np.asarray(bqkv, f32)
    d = dict(
        wqkv=wqkv,
        bqkv_r=np.ascontiguousarray(bq.reshape(48, 128).T),
        bqkv_f=np.ascontiguousarray(bq.reshape(1, 3 * H)),
        wo=np.ascontiguousarray(np.asarray(Wo, f32)),
        bo_f=np.ascontiguousarray(np.asarray(bo, f32).reshape(1, H)),
        gs=np.ascontiguousarray(np.concatenate(
            [np.asarray(gate_w, f32).T, np.asarray(sgate_w, f32).T], axis=1)),
    )
    sgu = np.asarray(sgu, f32)
    sdn = np.asarray(sdn, f32)
    w13 = np.asarray(w13, f32)
    w2 = np.asarray(w2, f32)
    w13a = np.empty((NE, H, 2816), f32)
    w2a = np.empty((NE, 1408, H), f32)
    for v in range(4):
        w13a[v, :, 0:1408] = sgu[1408 * v:1408 * (v + 1), :].T
        w13a[v, :, 1408:2816] = sgu[5632 + 1408 * v:5632 + 1408 * (v + 1), :].T
        w2a[v] = sdn[:, 1408 * v:1408 * (v + 1)].T
    for e in range(16):
        w13a[4 + e] = w13[e].T
        w2a[4 + e] = w2[e].T
    d["w13a"] = np.ascontiguousarray(
        w13a.reshape(NE, 16, 128, 22, 128).transpose(0, 3, 2, 1, 4))
    d["w2a"] = np.ascontiguousarray(w2a)
    d["wqk"] = np.ascontiguousarray(
        wqkv.reshape(16, 128, 48, 128).transpose(2, 1, 0, 3))
    jl = np.arange(128, dtype=f32)[:, None]
    il = np.arange(512, dtype=f32)[None, :]
    d4 = np.empty((128, 2048), f32)
    m4 = np.empty((128, 2048), f32)
    for m in range(4):
        dd = jl + 128 * m - il
        d4[:, 512 * m:512 * (m + 1)] = np.where(dd <= 0, dd, NEGD)
        m4[:, 512 * m:512 * (m + 1)] = np.where(dd <= 0, 0.0, NEGD)
    d["d4"] = np.ascontiguousarray(d4)
    d["m4"] = np.ascontiguousarray(m4)
    return d


def _prep_core(x, b, r):
    f32 = np.float32
    xb = np.asarray(x[b], f32)
    pmv = np.zeros((128, 32), f32)
    if r == 0:
        xc = np.concatenate([np.zeros((OWN0, H), f32), xb[:OWN0]], axis=0)
        pmv[:, 0:16] = -1.0e30
    else:
        xc = xb
    return np.ascontiguousarray(xc), pmv


def kernel(L0_attn_x, L0_attn_Wqkv, L0_attn_bqkv, L0_attn_Wo, L0_attn_bo,
           L0_ffn_gate_w, L0_ffn_sgate_w, L0_ffn_w13, L0_ffn_w2,
           L0_ffn_shared_gate_up_w, L0_ffn_shared_down_w):
    shared = _prep_shared(L0_attn_Wqkv, L0_attn_bqkv, L0_attn_Wo, L0_attn_bo,
                          L0_ffn_gate_w, L0_ffn_sgate_w, L0_ffn_w13, L0_ffn_w2,
                          L0_ffn_shared_gate_up_w, L0_ffn_shared_down_w)
    nc = build_nc(debug=DEBUG)
    in_maps = []
    for c in range(8):
        b, r = c // 2, c % 2
        xc, pmv = _prep_core(L0_attn_x, b, r)
        m = dict(shared)
        m["x"] = xc
        m["pm"] = pmv
        in_maps.append(m)
    res = run_bass_kernel_spmd(nc, in_maps, core_ids=list(range(8)))
    out = np.empty((4, SEQ, H), np.float32)
    for c in range(8):
        b, r = c // 2, c % 2
        out[b, OWN0 * r:OWN0 * (r + 1), :] = res.results[c]["out"]
    return out


# version 16
# speedup vs baseline: 1.1335x; 1.1335x over previous
import math
import contextlib
import numpy as np
import ml_dtypes
import concourse.bass as bass
import concourse.mybir as mybir
from concourse.bass_utils import run_bass_kernel_spmd

F32 = mybir.dt.float32
BF16 = mybir.dt.bfloat16
AF = mybir.ActivationFunctionType
ALU = mybir.AluOpType
AX = mybir.AxisListType

H = 2048
NH = 16
SEQ = 4096
OWN0 = 2048
NMG = 4
NE = 20
ISQ = 1.0 / math.sqrt(128.0)
EPS = 1e-5
SLOPES = [2.0 ** (-k / 2.0) for k in range(NH)]
WH = [int(math.ceil(45.0 / s)) for s in SLOPES]
NEGD = -1.0e33
COLF = 7

DEBUG = None
ABLATE = frozenset()


def _jr(qt, h):
    qend = OWN0 + 512 * (qt + 1)
    js = max(0, ((qend - 512 - WH[h]) // 128) * 128)
    return js, (qend - js) // 128

ENGS = ("sync", "scalar", "vector", "tensor", "gpsimd")


class Sem:
    __slots__ = ("h", "n", "name")
    def __init__(self, h, name):
        self.h, self.n, self.name = h, 0, name


class Prog:
    CHAIN = ("vector", "scalar", "gpsimd")

    def __init__(self):
        self.items = {e: [] for e in ENGS}
        self.eng_sem = {}
        self.last = {}

    def op(self, eng, fn, waits=(), inc=None, inck=1):
        auto = ()
        if fn is not None and eng in self.CHAIN:
            es = self.eng_sem[eng]
            if inc is None:
                inc, inck = es, 1
            prev = self.last.get(eng)
            if prev is not None:
                auto = (prev,)
            if inc is es:
                self.last[eng] = (es, es.n + inck)
        tok = None
        if inc is not None:
            inc.n += inck
            tok = (inc, inc.n)
        ws = []
        seen = {}
        for wt in tuple(waits) + auto:
            if wt is None:
                continue
            s, v = wt
            assert v <= s.n, f"wait on future token {s.name} {v} > {s.n}"
            if v > 0 and seen.get(id(s), -1) < v:
                seen[id(s)] = v
        ws = [(s, v) for wt in [None] for s, v in []]
        done = set()
        out = []
        for wt in tuple(waits) + auto:
            if wt is None:
                continue
            s, v = wt
            if id(s) in done:
                continue
            done.add(id(s))
            out.append((s, seen[id(s)]))
        out = [(s, v) for (s, v) in out if v > 0]
        self.items[eng].append((fn, tuple(out), inc, inck if inc else 0))
        return tok

    def wait(self, eng, sem, val):
        self.op(eng, None, waits=((sem, val),))

    def emit(self, block):
        for eng in ENGS:
            items = self.items[eng]
            if not items:
                continue
            def mk(items_):
                def run(q):
                    for fn, waits, inc, inck in items_:
                        for (s, v) in waits:
                            q.wait_ge(s.h, v)
                        ins = fn(q) if fn is not None else None
                        if inc is not None:
                            ins.then_inc(inc.h, inck)
                return run
            getattr(block, eng)(mk(items))
        self.items = {e: [] for e in ENGS}


def build_nc(debug=None):
    nc = bass.Bass("TRN2", target_bir_lowering=False)
    dbgA = debug == "A"
    dbgB = debug == "B"

    x_in = nc.dram_tensor("x", [SEQ, H], F32, kind="ExternalInput").ap()
    wqkv = nc.dram_tensor("wqkv", [H, 3 * H], F32, kind="ExternalInput").ap()
    bqkv_r = nc.dram_tensor("bqkv_r", [128, 48], F32, kind="ExternalInput").ap()
    bqkv_f = nc.dram_tensor("bqkv_f", [1, 3 * H], F32, kind="ExternalInput").ap()
    wo_in = nc.dram_tensor("wo", [H, H], F32, kind="ExternalInput").ap()
    bo_f = nc.dram_tensor("bo_f", [1, H], F32, kind="ExternalInput").ap()
    gs_in = nc.dram_tensor("gs", [H, 17], F32, kind="ExternalInput").ap()
    w13a = nc.dram_tensor("w13a", [NE, 22, 128, 16, 128], F32, kind="ExternalInput").ap()
    wqk = nc.dram_tensor("wqk", [48, 128, 16, 128], F32, kind="ExternalInput").ap()
    w2a = nc.dram_tensor("w2a", [NE, 1408, H], F32, kind="ExternalInput").ap()
    pm_in = nc.dram_tensor("pm", [128, 32], F32, kind="ExternalInput").ap()
    d4_in = nc.dram_tensor("d4", [128, 2048], F32, kind="ExternalInput").ap()
    m4_in = nc.dram_tensor("m4", [128, 2048], F32, kind="ExternalInput").ap()

    out_t = nc.dram_tensor("out", [OWN0, H], F32, kind="ExternalOutput").ap()
    skind = "ExternalOutput" if dbgA else "Internal"
    xnT_s = nc.dram_tensor("xnT_s", [H, OWN0], BF16, kind=skind).ap()
    k_s = nc.dram_tensor("k_s", [H, SEQ], BF16, kind=skind).ap()
    v_s = nc.dram_tensor("v_s", [H, SEQ], BF16, kind=skind).ap()
    q_s = nc.dram_tensor("q_s", [H, OWN0], BF16, kind=skind).ap()
    s_s = nc.dram_tensor("s_s", [OWN0, H], F32,
                         kind="ExternalOutput" if (dbgA or dbgB) else "Internal").ap()
    acc_s = nc.dram_tensor("acc_s", [OWN0, H], F32, kind="Internal").ap()
    if dbgA:
        dbg_coef = nc.dram_tensor("dbg_coef", [128, 16 * NE], F32,
                                  kind="ExternalOutput").ap()
        dbg_m = nc.dram_tensor("dbg_m", [1, 3 * NH], F32, kind="ExternalOutput").ap()

    stk = contextlib.ExitStack()
    def S_(name):
        return Sem(stk.enter_context(nc.semaphore(name)), name)

    p = Prog()
    with stk:
        sxA = [S_(f"sxA{i}") for i in range(3)]
        swk2 = [S_(f"swk{i}") for i in range(2)]
        swc = S_("swc"); swv = S_("swv"); swo = S_("swo")
        svear = [S_(f"sve{i}") for i in range(2)]
        skqr = [S_(f"skq{i}") for i in range(2)]
        sxn = S_("sxn")
        sser = [S_(f"sse{i}") for i in range(2)]
        skvr = [S_(f"skv{i}") for i in range(2)]
        sxch = S_("sxch")
        sepa = S_("sepa"); sepx = S_("sepx"); seps = S_("seps")
        sw3r = [S_(f"sw3{i}") for i in range(2)]
        sw2x = S_("sw2x")
        sacc8 = [S_(f"sacc{i}") for i in range(8)]
        sconv = S_("sconv")
        st = S_("st"); sv = S_("sv"); ss = S_("ss"); sg = S_("sg"); so = S_("so")
        ALL = (sxA + swk2 + [swc, swv, swo] + svear + skqr + [sxn] + sser + skvr
               + [sxch, sepa, sepx, seps] + sw3r + [sw2x] + sacc8
               + [sconv, st, sv, ss, sg, so])
        p.eng_sem = {"vector": sv, "scalar": ss, "gpsimd": sg, "tensor": st}
        swc_prev = [None]
        def CDMA(fn, extra=()):
            t = OP("sync", fn, waits=(swc_prev[0],) + tuple(extra),
                   inc=swc, inck=16)
            swc_prev[0] = t
            return t

        def T(name, shape, dt):
            return stk.enter_context(nc.sbuf_tensor(name, shape, dt))

        io_p = T("io_p", [128, 128], F32)
        io_c = T("io_c", [128, 128], F32)
        id_f = T("id_f", [128, 128], F32)
        id_b = T("id_b", [128, 128], BF16)
        ones_r = T("ones_r", [1, 128], F32)
        ones_cb = T("ones_cb", [128, 1], BF16)
        iota32 = T("iota32", [128, 32], F32)
        pm_t = T("pm_t", [128, 32], F32)
        gs_t = T("gs_t", [128, 16 * 17], F32)
        bq_r = T("bq_r", [128, 48], F32)
        bq_s = T("bq_s", [128, 16], F32)
        bvf = T("bvf", [1, H], F32)
        coef = T("coef", [128, 16 * NE], F32)
        mxk2 = T("mxk2", [1, NH], F32)
        mxq2 = T("mxq2", [1, NH], F32)
        m_h = T("m_h", [1, NH], F32)
        sm = T("smalls", [128, 64], F32)
        block = stk.enter_context(nc.Block())
        OP = p.op

        OP("gpsimd", lambda g: g.iota(io_p[:], pattern=[[0, 128]], base=0,
           channel_multiplier=1, allow_small_or_imprecise_dtypes=True), inc=sg)
        OP("gpsimd", lambda g: g.iota(io_c[:], pattern=[[1, 128]], base=0,
           channel_multiplier=0, allow_small_or_imprecise_dtypes=True), inc=sg)
        OP("gpsimd", lambda g: g.iota(iota32[:], pattern=[[128, 32]], base=0,
           channel_multiplier=1, allow_small_or_imprecise_dtypes=True), inc=sg)
        OP("gpsimd", lambda g: g.memset(ones_r[:], 1.0), inc=sg)
        OP("gpsimd", lambda g: g.memset(ones_cb[:], 1.0), inc=sg)
        OP("gpsimd", lambda g: g.memset(mxk2[:], 0.0), inc=sg)
        tok_ms = OP("gpsimd", lambda g: g.memset(mxq2[:], 0.0), inc=sg)
        tok_idf = OP("vector", lambda v: v.tensor_tensor(id_f[:], io_p[:], io_c[:],
                     ALU.is_equal), waits=((sg, 2),), inc=sv)
        tok_idb = OP("vector", lambda v: v.tensor_copy(id_b[:], id_f[:]), inc=sv)
        tok_pm = CDMA(lambda q: q.dma_start(out=pm_t[:], in_=pm_in))
        tok_cns = CDMA(lambda q: q.dma_start(
            out=gs_t[:].rearrange("p (c n) -> p c n", c=16),
            in_=gs_in.rearrange("(c p) n -> p c n", p=128)))
        tok_bq = CDMA(lambda q: q.dma_start(out=bq_r[:], in_=bqkv_r))
        tok_bvf = CDMA(lambda q: q.dma_start(out=bvf[:],
                       in_=bqkv_f[0:1, 2 * H:3 * H]))
        tok_bqs = OP("vector", lambda v: v.tensor_scalar_mul(bq_s[:], bq_r[:, 0:16],
                     ISQ), waits=(tok_bq,), inc=sv)

        def _phaseA(xb, sqd, stats, xn_b, xn_f, xnT_f, xnT_mg, wkb,
                    kqe, ksq, rt, rtt, ptb, ptf, pa):
            pab = [pa[:, i * 512:(i + 1) * 512] for i in range(6)]
            xnT_2 = xnT_mg[:].rearrange("p (r c t) -> p r c t", r=3, c=16)

            t_norm, t_sq, t_ssq, t_trd = {}, {}, {}, {}
            ttoks = {}
            tok_xnT_dma = {}
            tok_prj_done = {}
            ptb_free = tok_idb
            ptf_free = [tok_idf]
            xnf_free = {}
            rt_prev = None
            wkb_free = [None, None]
            kqe_free = [None, None]
            pa03_free = []
            pa4_free = [None]
            pa5_free = [None]

            def emit_tiles(mg):
                nonlocal ptb_free, ptf_free, rt_prev, pa5_free
                own = mg >= 2
                xnT_v = xnT_2[:, mg % 3]
                for tl in range(8):
                    tt = 8 * mg + tl
                    xs = tt % 3
                    xt = xb[:, xs * H:(xs + 1) * H]
                    w = []
                    if tt >= 3:
                        w = [(sv, t_sq[tt - 3][1]), (ss, t_norm[tt - 3][1])]
                    tok_x = OP("sync", lambda q, tt=tt, xt=xt: q.dma_start(
                        out=xt, in_=x_in[128 * tt:128 * (tt + 1), :]),
                        waits=tuple(w), inc=sxA[tt % 3], inck=16)
                    sb = (tt % 2) * 8
                    s1 = stats[:, sb:sb + 1]; ssqc = stats[:, sb + 1:sb + 2]
                    mean = stats[:, sb + 2:sb + 3]; msq = stats[:, sb + 3:sb + 4]
                    var = stats[:, sb + 4:sb + 5]; lv = stats[:, sb + 5:sb + 6]
                    rstd = stats[:, sb + 6:sb + 7]; nb = stats[:, sb + 7:sb + 8]
                    w2_ = (ss, t_norm[tt - 2][1]) if tt >= 2 else None
                    t_sq[tt] = OP("vector", lambda v, xt=xt, s1=s1: v.reduce_sum(
                        s1, xt, axis=AX.X), waits=(tok_x, w2_), inc=sv)
                    t_ssq[tt] = OP("scalar", lambda s, xt=xt, ssqc=ssqc:
                        s.activation(sqd[:], xt, AF.Square, accum_out=ssqc),
                        waits=(tok_x, w2_), inc=ss)
                    OP("vector", lambda v, s1=s1, mean=mean: v.tensor_scalar_mul(
                        mean, s1, 1.0 / H), inc=sv)
                    OP("vector", lambda v, mean=mean, msq=msq: v.tensor_mul(
                        msq, mean, mean), inc=sv)
                    OP("vector", lambda v, msq=msq: v.tensor_scalar(
                        msq, msq, EPS, None, ALU.subtract), inc=sv)
                    tok_var = OP("vector", lambda v, ssqc=ssqc, msq=msq, var=var:
                        v.scalar_tensor_tensor(var, ssqc, 1.0 / H, msq, ALU.mult,
                        ALU.subtract), waits=(t_ssq[tt],), inc=sv)
                    OP("scalar", lambda s, var=var, lv=lv: s.activation(
                        lv, var, AF.Ln), waits=(tok_var,), inc=ss)
                    tok_rstd = OP("scalar", lambda s, lv=lv, rstd=rstd: s.activation(
                        rstd, lv, AF.Exp, scale=-0.5), inc=ss)
                    tok_nb = OP("vector", lambda v, mean=mean, rstd=rstd, nb=nb:
                        v.scalar_tensor_tensor(nb, mean, -1.0, rstd, ALU.mult,
                        ALU.mult), waits=(tok_rstd,), inc=sv)
                    prev_same = tt - 1 if (tt - 1 >= 0 and (tt - 1 >= 16) == own) \
                        else None
                    wnorm = (st, t_trd[prev_same]) if prev_same is not None else None
                    if own:
                        dstn = xn_f[:, 0:H]
                    else:
                        dstn = xn_b[:, 0:H]
                    t_norm[tt] = OP("scalar", lambda s, xt=xt, o=dstn, rstd=rstd,
                        nb=nb: s.activation(o, xt, AF.Identity, bias=nb, scale=rstd),
                        waits=(tok_nb, wnorm), inc=ss)

                    wmg = []
                    if tl == 0 and mg > 2:
                        wmg.append((st, tok_prj_done[mg - 3]))
                    if 'tr' in ABLATE:
                        t_trd[tt] = st.n
                        continue
                    if own:
                        src = xn_f[:, 0:H]
                        for b4 in range(4):
                            for c4 in range(4):
                                c = 4 * b4 + c4
                                wtr = [t_norm[tt]]
                                if c4 == 0:
                                    wtr += [tok_idf] + list(ptf_free) + wmg
                                    wmg = []
                                tok_tr = OP("tensor", lambda t, src=src, c=c, c4=c4:
                                    t.transpose(ptf[:, 128 * c4:128 * (c4 + 1)],
                                    src[:, 128 * c:128 * (c + 1)], id_f[:]),
                                    waits=tuple(wtr), inc=st)
                            wev = [tok_tr]
                            if rt_prev is not None:
                                wev.append(rt_prev)
                            tok_evf = OP("vector", lambda v, b4=b4: v.tensor_copy(
                                xnT_f[:, 512 * b4:512 * (b4 + 1)], ptf[:]),
                                waits=tuple(wev), inc=sv)
                            tok_evs = OP("scalar", lambda s, b4=b4, tl=tl,
                                xnT_v=xnT_v: s.activation(
                                xnT_v[:, 4 * b4:4 * b4 + 4, 128 * tl:128 * (tl + 1)],
                                ptf[:].rearrange("p (c f) -> p c f", c=4), AF.Copy),
                                waits=(tok_tr, tok_evf), inc=ss)
                            ptf_free = [tok_evf, tok_evs]
                        t_trd[tt] = st.n
                        for c in (range(16) if 'router' not in ABLATE else []):
                            wrt = []
                            if c == 0:
                                wrt = [tok_evf, tok_cns] + pa5_free
                            OP("tensor", lambda t, c=c: t.matmul(
                                pab[5][:, 0:17], xnT_f[:, 128 * c:128 * (c + 1)],
                                gs_t[:].rearrange("p (c n) -> p c n", c=16)[:, c, :],
                                start=(c == 0), stop=(c == 15)),
                                waits=tuple(wrt), inc=st)
                        rt_prev = (st, st.n) if 'router' not in ABLATE else rt_prev
                        skip_topk = 'router' in ABLATE
                        ttg = tt - 16
                        cslab = coef[:, NE * ttg:NE * (ttg + 1)]
                        m1 = sm[:, 0:1]; negm = sm[:, 1:2]; sumw = sm[:, 2:3]
                        rsum = sm[:, 3:4]; t4 = sm[:, 4:5]
                        if skip_topk:
                            continue
                        tok_rtc = OP("vector", lambda v: v.tensor_copy(rt[:, 0:17],
                           pab[5][:, 0:17]), waits=(rt_prev,), inc=sv)
                        pa5_free = [tok_rtc]
                        OP("vector", lambda v: v.reduce_max(m1, rt[:, 0:16],
                           axis=AX.X), inc=sv)
                        tok_negm = OP("vector", lambda v: v.tensor_scalar_mul(
                            negm, m1, -1.0), inc=sv)
                        tok_th = OP("scalar", lambda s: s.activation(
                            rtt[:, 16:17], rt[:, 16:17], AF.Tanh, scale=0.5),
                            waits=(tok_rtc,), inc=ss)
                        tok_exp = OP("scalar", lambda s: s.activation(
                            rt[:, 17:33], rt[:, 0:16], AF.Exp, bias=negm,
                            accum_out=sumw), waits=(tok_negm,), inc=ss)
                        OP("vector", lambda v: v.tensor_copy(rtt[:, 0:16],
                           rt[:, 0:16]), waits=(tok_exp,), inc=sv)
                        for _ in range(3):
                            OP("vector", lambda v: v.reduce_max(t4, rtt[:, 0:16],
                               axis=AX.X), inc=sv)
                            OP("vector", lambda v: v.tensor_scalar(
                               sm[:, 6:22], rtt[:, 0:16], t4, None,
                               ALU.is_equal), inc=sv)
                            OP("vector", lambda v: v.scalar_tensor_tensor(
                               rtt[:, 0:16], sm[:, 6:22], -1e30, rtt[:, 0:16],
                               ALU.mult, ALU.add), inc=sv)
                        OP("vector", lambda v: v.reduce_max(t4, rtt[:, 0:16],
                           axis=AX.X), inc=sv)
                        OP("vector", lambda v: v.tensor_scalar(
                           sm[:, 6:22], rt[:, 0:16], t4, None, ALU.is_ge), inc=sv)
                        OP("vector", lambda v: v.reciprocal(rsum, sumw), inc=sv)
                        OP("vector", lambda v, cslab=cslab: v.tensor_tensor(
                           cslab[:, 4:20], rt[:, 17:33], sm[:, 6:22], ALU.mult),
                           inc=sv)
                        OP("vector", lambda v, cslab=cslab: v.tensor_scalar_mul(
                           cslab[:, 4:20], cslab[:, 4:20], rsum), inc=sv)
                        for c4 in range(4):
                            OP("vector", lambda v, cslab=cslab, c4=c4:
                               v.tensor_scalar(cslab[:, c4:c4 + 1], rtt[:, 16:17],
                               0.5, 0.5, ALU.mult, ALU.add),
                               waits=(tok_th,) if c4 == 0 else (), inc=sv)
                    else:
                        src = xn_b[:, 0:H]
                        for b8 in range(2):
                            for c8 in range(8):
                                c = 8 * b8 + c8
                                wtr = [t_norm[tt]]
                                if c8 == 0:
                                    wtr += [tok_idb, ptb_free] + wmg
                                    wmg = []
                                tok_tr = OP("tensor", lambda t, src=src, c=c, c8=c8:
                                    t.transpose(ptb[:, 128 * c8:128 * (c8 + 1)],
                                    src[:, 128 * c:128 * (c + 1)], id_b[:]),
                                    waits=tuple(wtr), inc=st)
                            tok_evb = OP("vector", lambda v, b8=b8, tl=tl,
                                xnT_v=xnT_v: v.tensor_copy(
                                xnT_v[:, 8 * b8:8 * b8 + 8, 128 * tl:128 * (tl + 1)],
                                ptb[:].rearrange("p (c f) -> p c f", c=8)),
                                waits=(tok_tr,), inc=sv)
                            ptb_free = tok_evb
                        t_trd[tt] = st.n
                    yield tl

                ttoks[mg] = ((sv, sv.n), (ss, ss.n))

            def emit_proj(mg, tok_xnT_v, tok_xnT_s, filler=None):
                nonlocal pa03_free, pa4_free
                own = mg >= 2
                xnT_v = xnT_2[:, mg % 3]
                for proj in ((["k", "q", "v"] if own else ["k", "v"])
                             if 'kq' not in ABLATE else []):
                    dst = {"k": k_s, "q": q_s, "v": v_s}[proj]
                    dcol = 1024 * (mg - 2) if proj == "q" else 1024 * mg
                    mx = mxk2 if proj == "k" else mxq2
                    for fc in range(16):
                        slot = fc % 2
                        qk_idx = {"q": fc, "k": 16 + fc, "v": 32 + fc}[proj]
                        tok_w = OP("gpsimd", lambda g, slot=slot, qk_idx=qk_idx:
                            g.dma_start(
                            out=wkb[:, slot * H:(slot + 1) * H]
                            .rearrange("p (kc f) -> p kc f", kc=16),
                            in_=wqk[qk_idx]),
                            waits=(wkb_free[slot],), inc=swk2[slot], inck=16)
                        for k in range(16):
                            wm = []
                            if k == 0:
                                wm = [tok_w, tok_xnT_v, tok_xnT_s]
                                if fc in (0, 1):
                                    wm += pa03_free
                                else:
                                    wm.append(kqe_evtok[slot])
                            for tk in range(2):
                                OP("tensor", lambda t, slot=slot, k=k, tk=tk,
                                    xnT_v=xnT_v: t.matmul(pab[2 * slot + tk],
                                    wkb[:, slot * H + 128 * k:
                                         slot * H + 128 * (k + 1)],
                                    xnT_v[:, k, 512 * tk:512 * (tk + 1)],
                                    start=(k == 0), stop=(k == 15)),
                                    waits=tuple(wm) if tk == 0 else (), inc=st)
                        tok_mm = (st, st.n)
                        wkb_free[slot] = tok_mm
                        if proj == "q":
                            bias_ap, scale = bq_s[:, fc:fc + 1], ISQ
                        elif proj == "k":
                            bias_ap, scale = bq_r[:, 16 + fc:17 + fc], 1.0
                        else:
                            bias_ap, scale = bq_r[:, 32 + fc:33 + fc], 1.0
                        tok_ev = OP("scalar", lambda s, slot=slot, bias_ap=bias_ap,
                            scale=scale: s.activation(
                            kqe[:, slot * 1024:(slot + 1) * 1024],
                            pa[:, 1024 * slot:1024 * (slot + 1)], AF.Identity,
                            bias=bias_ap, scale=scale),
                            waits=(tok_mm, kqe_free[slot], tok_bqs), inc=ss)
                        if fc == 0:
                            kqe_evtok = {}
                        kqe_evtok[slot] = tok_ev
                        if proj != "v":
                            tok_ksq = OP("scalar", lambda s, slot=slot:
                                s.activation(
                                ksq[:], kqe[:, slot * 1024:(slot + 1) * 1024],
                                AF.Square), waits=(tok_ev,), inc=ss)
                            for tk in range(2):
                                OP("tensor", lambda t, tk=tk: t.matmul(
                                    pab[4][0:1, :], ones_cb[:],
                                    ksq[:, 512 * tk:512 * (tk + 1)],
                                    start=(tk == 0), stop=(tk == 1)),
                                    waits=(tok_ksq, (sg, 5), pa4_free[0])
                                    if tk == 0 else (), inc=st)
                            tok_nmm = (st, st.n)
                            OP("vector", lambda v: v.reduce_max(sm[0:1, 8:9],
                               pab[4][0:1, :], axis=AX.X),
                               waits=(tok_nmm, (sg, tok_ms[1])), inc=sv)
                            pa4_free = [(sv, sv.n)]
                            OP("vector", lambda v, mx=mx, fc=fc: v.tensor_tensor(
                               mx[0:1, fc:fc + 1], mx[0:1, fc:fc + 1],
                               sm[0:1, 8:9], ALU.max), inc=sv)
                        kqe_free[slot] = OP("sync", lambda q, dst=dst, fc=fc,
                            dcol=dcol, slot=slot: q.dma_start(
                            out=dst[128 * fc:128 * (fc + 1), dcol:dcol + 1024],
                            in_=kqe[:, slot * 1024:(slot + 1) * 1024]),
                            waits=(tok_ev,), inc=skqr[slot], inck=16)
                        if filler is not None and fc % 4 == 3:
                            next(filler, None)
                    pa03_free = [kqe_evtok[0], kqe_evtok[1]]

                if filler is not None:
                    for _ in filler:
                        pass
                tok_prj_done[mg] = st.n

                if own:
                    tok_xnT_dma[mg] = OP("sync", lambda q, mg=mg, xnT_v=xnT_v:
                        q.dma_start(
                        out=xnT_s.rearrange("(c p) t -> p c t", p=128)
                        [:, :, 1024 * (mg - 2):1024 * (mg - 1)],
                        in_=xnT_v[:, :, :]),
                        waits=(tok_xnT_v, tok_xnT_s,
                               tok_xnT_dma.get(mg - 1)), inc=sxn, inck=16)


            for _ in emit_tiles(0):
                pass
            for _ in emit_tiles(1):
                pass
            emit_proj(0, *ttoks[0], filler=emit_tiles(2))
            emit_proj(1, *ttoks[1], filler=emit_tiles(3))
            emit_proj(2, *ttoks[2])
            emit_proj(3, *ttoks[3])

            if 'mh' in ABLATE:
                return [(s_, s_.n) for s_ in ALL if s_.n > 0], None
            OP("scalar", lambda s: s.activation(sm[0:1, 16:32], mxk2[:], AF.Ln),
               waits=((sv, sv.n),), inc=ss)
            tok_lnq = OP("scalar", lambda s: s.activation(sm[0:1, 32:48], mxq2[:],
                         AF.Ln), inc=ss)
            tok_mhv = OP("vector", lambda v: v.tensor_add(sm[0:1, 48:64],
                         sm[0:1, 16:32], sm[0:1, 32:48]), waits=(tok_lnq,), inc=sv)
            tok_mh = OP("scalar", lambda s: s.activation(m_h[:], sm[0:1, 48:64],
                        AF.Exp, scale=0.5), waits=(tok_mhv,), inc=ss)

            if dbgA:
                CDMA(lambda q: q.dma_start(out=dbg_coef, in_=coef[:]),
                     extra=((sv, sv.n),))
                OP("vector", lambda v: v.tensor_copy(sm[0:1, 16:32], mxq2[:]),
                   waits=(tok_mh,), inc=sv)
                OP("vector", lambda v: v.tensor_copy(sm[0:1, 32:48], mxk2[:]),
                   inc=sv)
                tok_dm = OP("vector", lambda v: v.tensor_copy(sm[0:1, 48:64],
                            m_h[:]), inc=sv)
                CDMA(lambda q: q.dma_start(out=dbg_m, in_=sm[0:1, 16:64]),
                     extra=(tok_dm,))

            return [(s_, s_.n) for s_ in ALL if s_.n > 0], tok_mh

        with contextlib.ExitStack() as stkA:
            xb = stkA.enter_context(nc.sbuf_tensor("xb", [128, 3 * H], F32))
            sqd = stkA.enter_context(nc.sbuf_tensor("sqd", [128, H], BF16))
            stats = stkA.enter_context(nc.sbuf_tensor("stats", [128, 16], F32))
            xn_b = stkA.enter_context(nc.sbuf_tensor("xn_b", [128, H], BF16))
            xn_f = stkA.enter_context(nc.sbuf_tensor("xn_f", [128, H], F32))
            xnT_f = stkA.enter_context(nc.sbuf_tensor("xnT_f", [128, H], F32))
            xnT_mg = stkA.enter_context(nc.sbuf_tensor("xnT_mg", [128, 3 * 16 * 1024], BF16))
            wkb = stkA.enter_context(nc.sbuf_tensor("wkb", [128, 2 * H], BF16))
            kqe = stkA.enter_context(nc.sbuf_tensor("kqe", [128, 2 * 1024], BF16))
            ksq = stkA.enter_context(nc.sbuf_tensor("ksq", [128, 1024], BF16))
            rt = stkA.enter_context(nc.sbuf_tensor("rt", [128, 40], F32))
            rtt = stkA.enter_context(nc.sbuf_tensor("rtt", [128, 24], F32))
            ptb = stkA.enter_context(nc.psum_tensor("ptb", [128, 1024], BF16))
            ptf = stkA.enter_context(nc.psum_tensor("ptf", [128, 512], F32))
            pa = stkA.enter_context(nc.psum_tensor("pa", [128, 6 * 512], F32))
            A_end, tok_mh = _phaseA(xb, sqd, stats, xn_b, xn_f, xnT_f, xnT_mg,
                                    wkb, kqe, ksq, rt, rtt, ptb, ptf, pa)

        if dbgA:
            for eng in ENGS:
                for w in A_end:
                    p.wait(eng, *w)
            p.emit(block)
            return nc

        def _phaseB(kb, vb, qb, pT, sdg, biasb, pmh, augr, rlx, bcs, afm,
                    wo_sb, bob, s_e, pb, pl, pm1, vt, pvt, iota_q, d4_t, m4_t,
                    bof, mhb, tok_mh):
            conv_tok = {e: None for e in range(NE)}
            pbb = [pb[:, i * 512:(i + 1) * 512] for i in range(4)]
            for eng in ENGS:
                for w in A_end:
                    p.wait(eng, *w)

            tok_wo = OP("gpsimd", lambda g: g.dma_start(
                out=wo_sb[:].rearrange("p (c f) -> p c f", c=16),
                in_=wo_in.rearrange("(c p) f -> p c f", p=128)), inc=swo, inck=16)
            OP("gpsimd", lambda g: g.iota(iota_q[:], pattern=[[1, 2048]],
               base=OWN0, channel_multiplier=0,
               allow_small_or_imprecise_dtypes=True), inc=sg)
            tok_iq = (sg, sg.n)
            CDMA(lambda q: q.dma_start(out=d4_t[:], in_=d4_in))
            tok_d4 = CDMA(lambda q: q.dma_start(out=m4_t[:], in_=m4_in))
            tok_bof = CDMA(lambda q: q.dma_start(out=bof[:], in_=bo_f))
            for v4 in range(4):
                OP("tensor", lambda t, v4=v4: t.matmul(
                    pbb[v4], ones_r[:], bof[0:1, 512 * v4:512 * (v4 + 1)],
                    start=True, stop=True), waits=(tok_bof,) if v4 == 0 else (),
                    inc=st)
            tok_bob = OP("vector", lambda v: v.tensor_copy(bob[:], pb[:, 0:2048]),
                         waits=((st, st.n),), inc=sv)
            tok_mb = OP("tensor", lambda t: t.matmul(pm1[:, 0:16], ones_r[:],
                m_h[0:1, 0:16], start=True, stop=True),
                waits=(tok_mh, tok_bob), inc=st)
            tok_mhb = OP("vector", lambda v: v.tensor_copy(mhb[:], pm1[:, 0:16]),
                waits=(tok_mb,), inc=sv)

            kb_free = [None, None]; vb_free = [None, None]; qb_free = [None, None]
            pT_free = [None] * 4
            vt_free = [None] * 4
            pvt_free = [None] * 2
            pbS_free = [tok_bob, tok_mhb]
            pb2_free = tok_bob
            pb3_free = tok_bob
            pl_free = None
            pmh_rd = None
            aug_rd = [None, None]
            afm_rd = None
            s_e_free = [None, None]
            sdg_rd = [None, None]

            for qt in range(4):
                qend = OWN0 + 512 * (qt + 1)
                for h in range(NH):
                    js, nj = _jr(qt, h)
                    slot = h % 2
                    colf = h >= COLF
                    OP("sync", lambda q, slot=slot, h=h, js=js, qend=qend:
                        q.dma_start(out=kb[:, slot * SEQ:slot * SEQ + (qend - js)],
                        in_=k_s[128 * h:128 * (h + 1), js:qend]),
                        waits=(kb_free[slot],), inc=skvr[slot], inck=16)
                    OP("sync", lambda q, slot=slot, h=h, js=js, qend=qend:
                        q.dma_start(out=vb[:, slot * SEQ:slot * SEQ + (qend - js)],
                        in_=v_s[128 * h:128 * (h + 1), js:qend]),
                        waits=(vb_free[slot],), inc=skvr[slot], inck=16)
                    tok_qb = OP("sync", lambda q, slot=slot, h=h, qt=qt: q.dma_start(
                        out=qb[:, slot * 512:(slot + 1) * 512],
                        in_=q_s[128 * h:128 * (h + 1), 512 * qt:512 * (qt + 1)]),
                        waits=(qb_free[slot],), inc=skvr[slot], inck=16)
                    tok_kb = tok_qb
                    tok_vb = tok_qb
                    if not colf:
                        tok_aug = OP("vector", lambda v, slot=slot, h=h, qt=qt:
                            v.tensor_scalar_mul(
                            augr[0:1, slot * 512:(slot + 1) * 512],
                            iota_q[0:1, 512 * qt:512 * (qt + 1)], -SLOPES[h]),
                            waits=(aug_rd[slot], tok_iq), inc=sv)
                    off = -SLOPES[h] * (qend - 1) if colf else 0.0
                    tok_negm2 = OP("vector", lambda v, off=off, h=h: v.tensor_scalar(
                        sm[:, 9:10], mhb[:, h:h + 1], -1.0, off, ALU.mult, ALU.add),
                        waits=(tok_mhb,), inc=sv)
                    tok_pmh = OP("vector", lambda v: v.tensor_scalar(
                        pmh[:], pm_t[:], sm[:, 9:10], None, ALU.add),
                        waits=(pmh_rd,), inc=sv)

                    expS = {}
                    vte_toks = {}
                    Sbanks = [pbb[0], pbb[1], pm1[:]]

                    def emit_av(j):
                        pslot_ = j % 4
                        vslot_ = j % 4
                        wA = [expS[j], vte_toks[j]]
                        if j == 0:
                            wA += [pb2_free]
                        OP("tensor", lambda t, vslot=vslot_, j=j, pslot=pslot_,
                            nj=nj: t.matmul(pbb[2],
                            vt[:, 128 * vslot:128 * (vslot + 1)],
                            pT[:, 512 * pslot:512 * (pslot + 1)],
                            start=(j == 0), stop=(j == nj - 1)),
                            waits=tuple(wA), inc=st)
                        pT_free[pslot_] = OP("tensor", lambda t, pslot=pslot_,
                            j=j, nj=nj: t.matmul(pl[:], ones_cb[:],
                            pT[:, 512 * pslot:512 * (pslot + 1)],
                            start=(j == 0), stop=(j == nj - 1)),
                            waits=(pl_free,) if j == 0 else (), inc=st)
                        vt_free[vslot_] = pT_free[pslot_]

                    for jt in range(nj):
                        JT = js // 128 + jt
                        diag = jt >= nj - 4
                        sbk = Sbanks[jt % 3]
                        wS = []
                        if jt < 3:
                            wS += pbS_free
                            wS += [tok_kb, tok_qb]
                        else:
                            wS.append(expS[jt - 3])
                        if (not colf) and (not diag):
                            OP("tensor", lambda t, sbk=sbk, slot=slot: t.matmul(
                                sbk, ones_r[:],
                                augr[0:1, slot * 512:(slot + 1) * 512],
                                start=True, stop=False),
                                waits=tuple(wS) + (tok_aug,))
                            tok_S = OP("tensor", lambda t, sbk=sbk, slot=slot,
                                jt=jt: t.matmul(sbk,
                                kb[:, slot * SEQ + 128 * jt:
                                    slot * SEQ + 128 * (jt + 1)],
                                qb[:, slot * 512:(slot + 1) * 512],
                                start=False, stop=True), inc=st)
                        else:
                            tok_S = OP("tensor", lambda t, sbk=sbk, slot=slot,
                                jt=jt: t.matmul(sbk,
                                kb[:, slot * SEQ + 128 * jt:
                                    slot * SEQ + 128 * (jt + 1)],
                                qb[:, slot * 512:(slot + 1) * 512],
                                start=True, stop=True), waits=tuple(wS), inc=st)
                        bslot = jt % 4
                        pslot = jt % 4
                        if colf or (not diag):
                            tok_bias = OP("vector", lambda v, bslot=bslot, JT=JT,
                                h=h: v.scalar_tensor_tensor(
                                biasb[:, bslot:bslot + 1], iota32[:, JT:JT + 1],
                                SLOPES[h], pmh[:, JT:JT + 1], ALU.mult, ALU.add),
                                waits=(tok_pmh,
                                       expS.get(jt - 4)), inc=sv)
                            bias_ap = biasb[:, bslot:bslot + 1]
                            bw = tok_bias
                        else:
                            bias_ap = pmh[:, JT:JT + 1]
                            bw = tok_pmh
                        if diag:
                            m4i = jt - (nj - 4)
                            ds_ = jt % 2
                            dt_ = m4_t if colf else d4_t
                            sc_ = 1.0 if colf else SLOPES[h]
                            tok_dg = OP("vector", lambda v, m4i=m4i, sbk=sbk,
                                dt_=dt_, sc_=sc_, ds_=ds_:
                                v.scalar_tensor_tensor(
                                sdg[:, 512 * ds_:512 * (ds_ + 1)],
                                dt_[:, 512 * m4i:512 * (m4i + 1)], sc_,
                                sbk, ALU.mult, ALU.add),
                                waits=(tok_S, sdg_rd[ds_], tok_d4), inc=sv)
                            expS[jt] = OP("scalar", lambda s, pslot=pslot,
                                bias_ap=bias_ap, ds_=ds_: s.activation(
                                pT[:, 512 * pslot:512 * (pslot + 1)],
                                sdg[:, 512 * ds_:512 * (ds_ + 1)],
                                AF.Exp, bias=bias_ap),
                                waits=(tok_dg, bw, pT_free[pslot]), inc=ss)
                            sdg_rd[ds_] = expS[jt]
                        else:
                            expS[jt] = OP("scalar", lambda s, pslot=pslot, sbk=sbk,
                                bias_ap=bias_ap: s.activation(
                                pT[:, 512 * pslot:512 * (pslot + 1)], sbk,
                                AF.Exp, bias=bias_ap),
                                waits=(tok_S, bw, pT_free[pslot]), inc=ss)
                        vslot = jt % 4
                        vbk = jt % 2
                        tok_vt = OP("tensor", lambda t, slot=slot, jt=jt,
                            vbk=vbk: t.transpose(
                            pvt[:, 1024 * vbk:1024 * vbk + 128],
                            vb[:, slot * SEQ + 128 * jt:
                                slot * SEQ + 128 * (jt + 1)], id_b[:]),
                            waits=(tok_vb, tok_idb, pvt_free[vbk]), inc=st)
                        vte_toks[jt] = OP("vector", lambda v, vslot=vslot,
                            vbk=vbk:
                            v.tensor_copy(vt[:, 128 * vslot:128 * (vslot + 1)],
                            pvt[:, 1024 * vbk:1024 * vbk + 128]),
                            waits=(tok_vt, vt_free[vslot]), inc=sv)
                        pvt_free[vbk] = vte_toks[jt]
                        if jt >= 3:
                            emit_av(jt - 3)
                    emit_av(nj - 3)
                    emit_av(nj - 2)
                    emit_av(nj - 1)
                    tok_sweep = (st, st.n)
                    kb_free[slot] = tok_sweep
                    vb_free[slot] = tok_sweep
                    qb_free[slot] = tok_sweep
                    aug_rd[slot] = tok_sweep
                    pmh_rd = (ss, ss.n)
                    tok_rl = OP("vector", lambda v: v.reciprocal(rlx[:], pl[:]),
                                waits=(tok_sweep,), inc=sv)
                    pl_free = tok_rl
                    tok_bc = OP("tensor", lambda t: t.matmul(pbb[3], ones_r[:],
                        rlx[:], start=True, stop=True),
                        waits=(tok_rl, pb3_free), inc=st)
                    tok_bcs = OP("vector", lambda v: v.tensor_copy(bcs[:], pbb[3]),
                        waits=(tok_bc,), inc=sv)
                    pb3_free = tok_bcs
                    tok_at = OP("vector", lambda v, h=h: v.tensor_tensor(
                        afm[:, 512 * h:512 * (h + 1)], pbb[2], bcs[:], ALU.mult),
                        waits=(tok_bcs, afm_rd), inc=sv)
                    pb2_free = tok_at
                    pbS_free = [(ss, ss.n)]
                    afm_done = tok_at
                tok_se = None
                for tl4 in range(4):
                    for c in range(16):
                        wW = []
                        if c == 0:
                            wW = [afm_done, tok_wo, pb2_free, pb3_free,
                                  s_e_free[tl4 % 2]]
                            wW += pbS_free
                            if tok_se is not None:
                                wW.append(tok_se)
                        for f4 in range(4):
                            OP("tensor", lambda t, c=c, tl4=tl4, f4=f4: t.matmul(
                                pbb[f4],
                                afm[:, 512 * c + 128 * tl4:
                                    512 * c + 128 * (tl4 + 1)],
                                wo_sb[:, H * c + 512 * f4:H * c + 512 * (f4 + 1)],
                                start=(c == 0), stop=(c == 15)),
                                waits=tuple(wW) if f4 == 0 else (), inc=st)
                    tok_womm = (st, st.n)
                    es = tl4 % 2
                    tok_se = OP("vector", lambda v, es=es: v.tensor_tensor(
                        s_e[:, es * H:(es + 1) * H], pb[:, 0:2048], bob[:],
                        ALU.add), waits=(tok_womm, tok_bob), inc=sv)
                    pbS_free = [tok_se]
                    pb2_free = tok_se
                    pb3_free = tok_se
                    s_e_free[es] = OP("sync", lambda q, qt=qt, tl4=tl4, es=es:
                        q.dma_start(out=s_s[512 * qt + 128 * tl4:
                        512 * qt + 128 * (tl4 + 1), :],
                        in_=s_e[:, es * H:(es + 1) * H]),
                        waits=(tok_se,), inc=sser[es], inck=16)
                afm_rd = (st, st.n)

            return [(s_, s_.n) for s_ in ALL if s_.n > 0], conv_tok

        with contextlib.ExitStack() as stkB:
            kb = stkB.enter_context(nc.sbuf_tensor("kb", [128, 2 * SEQ], BF16))
            vb = stkB.enter_context(nc.sbuf_tensor("vb", [128, 2 * SEQ], BF16))
            qb = stkB.enter_context(nc.sbuf_tensor("qb", [128, 2 * 512], BF16))
            pT = stkB.enter_context(nc.sbuf_tensor("pT", [128, 4 * 512], BF16))
            sdg = stkB.enter_context(nc.sbuf_tensor("sdg", [128, 1024], F32))
            biasb = stkB.enter_context(nc.sbuf_tensor("biasb", [128, 8], F32))
            pmh = stkB.enter_context(nc.sbuf_tensor("pmh", [128, 32], F32))
            augr = stkB.enter_context(nc.sbuf_tensor("augr", [1, 2 * 512], F32))
            rlx = stkB.enter_context(nc.sbuf_tensor("rlx", [1, 512], F32))
            bcs = stkB.enter_context(nc.sbuf_tensor("bcs", [128, 512], F32))
            afm = stkB.enter_context(nc.sbuf_tensor("afm", [128, 16 * 512], BF16))
            wo_sb = stkB.enter_context(nc.sbuf_tensor("wo_sb", [128, 16 * H], BF16))
            bob = stkB.enter_context(nc.sbuf_tensor("bob", [128, H], F32))
            s_e = stkB.enter_context(nc.sbuf_tensor("s_e", [128, 2 * H], F32))
            pb = stkB.enter_context(nc.psum_tensor("pb", [128, 4 * 512], F32))
            pl = stkB.enter_context(nc.psum_tensor("pl", [1, 512], F32))
            pm1 = stkB.enter_context(nc.psum_tensor("pm1", [128, 512], F32))
            vt = stkB.enter_context(nc.sbuf_tensor("vt", [128, 4 * 128], BF16))
            pvt = stkB.enter_context(nc.psum_tensor("pvt", [128, 2048], BF16))
            iota_q = stkB.enter_context(nc.sbuf_tensor("iota_q", [1, 2048], F32))
            d4_t = stkB.enter_context(nc.sbuf_tensor("d4_t", [128, 2048], F32))
            m4_t = stkB.enter_context(nc.sbuf_tensor("m4_t", [128, 2048], F32))
            bof = stkB.enter_context(nc.sbuf_tensor("bof", [1, H], F32))
            mhb = stkB.enter_context(nc.sbuf_tensor("mhb", [128, 16], F32))
            B_end, conv_tok = _phaseB(kb, vb, qb, pT, sdg, biasb, pmh, augr,
                                      rlx, bcs, afm, wo_sb, bob, s_e, pb, pl,
                                      pm1, vt, pvt, iota_q, d4_t, m4_t, bof,
                                      mhb, tok_mh)

        if dbgB:
            for eng in ENGS:
                for w in B_end:
                    p.wait(eng, *w)
            p.emit(block)
            return nc

        def _phaseC(xnh, w13b, w2b, hbuf, accst, silu_t, xe, se2, go, ot, oe, pc,
                    conv_tok):
            for eng in ENGS:
                for w in B_end:
                    p.wait(eng, *w)
            xnh_v = xnh[:].rearrange("p (c t) -> p c t", c=16)

            w13_free = [None, None]
            w2_rd = None
            xnh_rd = None
            pcg_free = [None, None]
            pcu_free = [None, None]
            pw2_free = [None, None]
            acc_dma = [None] * 8
            accst_free = [None] * 4
            pending_accum = []
            xe_free = [None, None]; se_free = [None, None]; oe_free = [None, None]
            ep_go_rd = None

            def emit_xn(hf):
                return OP("sync", lambda q, hf=hf: q.dma_start(
                    out=xnh_v[:, :, :],
                    in_=xnT_s.rearrange("(c p) t -> p c t", p=128)
                    [:, :, 1024 * hf:1024 * (hf + 1)]),
                    waits=(xnh_rd,), inc=sxch, inck=16)
            for hf in range(2):
                tok_xn = emit_xn(hf)
                for e in range(NE):
                    def emit_pair_dma(e, pi):
                        ps_ = pi % 2
                        OP("gpsimd", lambda g, e=e, pi=pi, ps_=ps_:
                            g.dma_start(
                            out=w13b[:, ps_ * 2 * H:ps_ * 2 * H + H]
                            .rearrange("p (k f) -> p k f", k=16),
                            in_=w13a[e, pi]),
                            waits=(w13_free[ps_], conv_tok[e]),
                            inc=sw3r[ps_], inck=16)
                        return OP("gpsimd", lambda g, e=e, pi=pi, ps_=ps_:
                            g.dma_start(
                            out=w13b[:, ps_ * 2 * H + H:(ps_ + 1) * 2 * H]
                            .rearrange("p (k f) -> p k f", k=16),
                            in_=w13a[e, pi + 11]),
                            inc=sw3r[ps_], inck=16)
                    tok_pair_dma = {0: emit_pair_dma(e, 0), 1: emit_pair_dma(e, 1)}

                    def emit_w2_dma(e, i):
                        return OP("gpsimd", lambda g, e=e, i=i: g.dma_start(
                            out=w2b[:, H * i:H * (i + 1)],
                            in_=w2a[e].rearrange("(ic p) f -> p ic f", p=128)
                            [:, i, :]),
                            waits=((w2_rd, conv_tok[e]) if i == 0 else ()),
                            inc=sw2x, inck=16)
                    tok_w2 = None
                    hb_last = None
                    for pi in range(11):
                        ps_ = pi % 2
                        tok_w3b = tok_pair_dma[pi]
                        tok_g = None
                        for piece in range(2):
                            base = ps_ * 2 * H + piece * H
                            pcof = (ps_ * 4 + piece * 2) * 512
                            wfree = pcg_free if piece == 0 else pcu_free
                            for k in range(16):
                                wm = []
                                if k == 0:
                                    wm = [tok_w3b, tok_xn, wfree[ps_]]
                                    if pi < 2:
                                        wm.append(pw2_free[ps_])
                                for tk in range(2):
                                    OP("tensor", lambda t, base=base, k=k, tk=tk,
                                        pcof=pcof: t.matmul(
                                        pc[:, pcof + 512 * tk:
                                           pcof + 512 * (tk + 1)],
                                        w13b[:, base + 128 * k:
                                             base + 128 * (k + 1)],
                                        xnh_v[:, k, 512 * tk:512 * (tk + 1)],
                                        start=(k == 0), stop=(k == 15)),
                                        waits=tuple(wm) if tk == 0 else (),
                                        inc=st)
                            if piece == 0:
                                tok_g = (st, st.n)
                        tok_pair = (st, st.n)
                        w13_free[ps_] = tok_pair
                        if pi + 2 <= 10:
                            tok_pair_dma[pi + 2] = emit_pair_dma(e, pi + 2)
                        if pi <= 8:
                            tok_w2 = emit_w2_dma(e, pi)
                            if pi >= 3 and pending_accum:
                                pending_accum.pop(0)()
                        gof = (ps_ * 4) * 512
                        uof = (ps_ * 4 + 2) * 512
                        tok_si = OP("scalar", lambda s, gof=gof: s.activation(
                            silu_t[:], pc[:, gof:gof + 1024], AF.Silu),
                            waits=(tok_g,), inc=ss)
                        pcg_free[ps_] = tok_si
                        tok_hb = OP("vector", lambda v, pi=pi, uof=uof:
                            v.tensor_tensor(hbuf[:, 1024 * pi:1024 * (pi + 1)],
                            silu_t[:], pc[:, uof:uof + 1024], ALU.mult),
                            waits=(tok_si, tok_pair), inc=sv)
                        pcu_free[ps_] = tok_hb
                        hb_last = tok_hb
                    tok_w2 = emit_w2_dma(e, 9)
                    tok_w2 = emit_w2_dma(e, 10)
                    for th in pending_accum:
                        th()
                    pending_accum = []
                    xnh_rd = (st, st.n)
                    for tl8 in range(8):
                        ttg = 8 * hf + tl8
                        ws_ = tl8 % 2
                        pof = ws_ * 4 * 512
                        for i in range(11):
                            wm = []
                            if i == 0:
                                wm = [tok_w2, hb_last, pw2_free[ws_]]
                            for f4 in range(4):
                                OP("tensor", lambda t, i=i, tl8=tl8, f4=f4,
                                    pof=pof: t.matmul(
                                    pc[:, pof + 512 * f4:pof + 512 * (f4 + 1)],
                                    hbuf[:, 1024 * i + 128 * tl8:
                                         1024 * i + 128 * (tl8 + 1)],
                                    w2b[:, H * i + 512 * f4:
                                        H * i + 512 * (f4 + 1)],
                                    start=(i == 0), stop=(i == 10)),
                                    waits=tuple(wm) if f4 == 0 else (), inc=st)
                        tok_w2mm = (st, st.n)
                        cc = coef[:, NE * ttg + e:NE * ttg + e + 1]
                        sl4 = tl8 % 4
                        tok_stg = OP("vector", lambda v, sl4=sl4, cc=cc, pof=pof:
                            v.tensor_scalar(
                            accst[:, sl4 * H:(sl4 + 1) * H],
                            pc[:, pof:pof + 2048], cc, None, ALU.mult),
                            waits=(tok_w2mm, accst_free[sl4]), inc=sv)
                        pw2_free[ws_] = tok_stg
                        def emit_accum(hf=hf, tl8=tl8, sl4=sl4, e=e,
                                       tok_stg=tok_stg):
                            prev = acc_dma[tl8]
                            if e == 0:
                                t_ = OP("gpsimd", lambda g: g.dma_start(
                                    out=acc_s[1024 * hf + 128 * tl8:
                                              1024 * hf + 128 * (tl8 + 1), :],
                                    in_=accst[:, sl4 * H:(sl4 + 1) * H]),
                                    waits=(tok_stg, prev), inc=sacc8[tl8],
                                    inck=16)
                            else:
                                t_ = OP("gpsimd", lambda g: g.dma_start(
                                    out=acc_s[1024 * hf + 128 * tl8:
                                              1024 * hf + 128 * (tl8 + 1), :],
                                    in_=accst[:, sl4 * H:(sl4 + 1) * H],
                                    accum_op=ALU.add),
                                    waits=(tok_stg, prev), inc=sacc8[tl8],
                                    inck=16)
                            acc_dma[tl8] = t_
                            accst_free[sl4] = t_
                        if tl8 < 4:
                            emit_accum()
                        else:
                            pending_accum.append(emit_accum)
                    w2_rd = (st, st.n)
                for th in pending_accum:
                    th()
                pending_accum = []
                for tl8 in range(8):
                    row = 1024 * hf + 128 * tl8
                    es = 0
                    tok_al = OP("sync", lambda q, row=row, es=es: q.dma_start(
                        out=accst[:, es * H:(es + 1) * H],
                        in_=acc_s[row:row + 128, :]),
                        waits=(acc_dma[tl8], accst_free[es]), inc=sepa, inck=16)
                    tok_x2 = OP("sync", lambda q, row=row, es=es: q.dma_start(
                        out=xe[:, es * H:(es + 1) * H],
                        in_=x_in[OWN0 + row:OWN0 + row + 128, :]),
                        waits=(xe_free[es],), inc=sepx, inck=16)
                    tok_s2 = OP("sync", lambda q, row=row, es=es: q.dma_start(
                        out=se2[:, es * H:(es + 1) * H],
                        in_=s_s[row:row + 128, :]),
                        waits=(se_free[es],), inc=seps, inck=16)
                    tok_gt = OP("scalar", lambda s, es=es: s.activation(
                        go[:], accst[:, es * H:(es + 1) * H], AF.Silu),
                        waits=(tok_al, ep_go_rd), inc=ss)
                    accst_free[es] = tok_gt
                    tok_sg = OP("vector", lambda v, es=es: v.tensor_tensor(
                        ot[:], go[:], se2[:, es * H:(es + 1) * H], ALU.mult),
                        waits=(tok_gt, tok_s2), inc=sv)
                    se_free[es] = tok_sg
                    tok_o = OP("vector", lambda v, es=es: v.tensor_tensor(
                        oe[:, es * H:(es + 1) * H], ot[:],
                        xe[:, es * H:(es + 1) * H], ALU.add),
                        waits=(tok_x2, oe_free[es]), inc=sv)
                    xe_free[es] = tok_o
                    ep_go_rd = tok_sg
                    oe_free[es] = OP("sync", lambda q, row=row, es=es: q.dma_start(
                        out=out_t[row:row + 128, :],
                        in_=oe[:, es * H:(es + 1) * H]),
                        waits=(tok_o,), inc=so, inck=16)

            p.wait("sync", so, so.n)

        with contextlib.ExitStack() as stkC:
            xnh = stkC.enter_context(nc.sbuf_tensor("xnh", [128, 16 * 1024], BF16))
            w13b = stkC.enter_context(nc.sbuf_tensor("w13b", [128, 4 * H], BF16))
            w2b = stkC.enter_context(nc.sbuf_tensor("w2b", [128, 11 * H], BF16))
            hbuf = stkC.enter_context(nc.sbuf_tensor("hbuf", [128, 11 * 1024], BF16))
            accst = stkC.enter_context(nc.sbuf_tensor("accst", [128, 4 * H], F32))
            silu_t = stkC.enter_context(nc.sbuf_tensor("silu_t", [128, 1024], F32))
            xe = stkC.enter_context(nc.sbuf_tensor("xe", [128, H], F32))
            se2 = stkC.enter_context(nc.sbuf_tensor("se2", [128, H], F32))
            go = stkC.enter_context(nc.sbuf_tensor("go", [128, H], F32))
            ot = stkC.enter_context(nc.sbuf_tensor("ot", [128, H], F32))
            oe = stkC.enter_context(nc.sbuf_tensor("oe", [128, H], F32))
            pc = stkC.enter_context(nc.psum_tensor("pc", [128, 8 * 512], F32))
            _phaseC(xnh, w13b, w2b, hbuf, accst, silu_t, xe, se2, go, ot, oe, pc,
                    conv_tok)
            p.emit(block)
    return nc


def _prep_shared(Wqkv, bqkv, Wo, bo, gate_w, sgate_w, w13, w2, sgu, sdn):
    f32 = np.float32
    wqkv = np.ascontiguousarray(np.asarray(Wqkv, f32))
    bq = np.asarray(bqkv, f32)
    d = dict(
        wqkv=wqkv,
        bqkv_r=np.ascontiguousarray(bq.reshape(48, 128).T),
        bqkv_f=np.ascontiguousarray(bq.reshape(1, 3 * H)),
        wo=np.ascontiguousarray(np.asarray(Wo, f32)),
        bo_f=np.ascontiguousarray(np.asarray(bo, f32).reshape(1, H)),
        gs=np.ascontiguousarray(np.concatenate(
            [np.asarray(gate_w, f32).T, np.asarray(sgate_w, f32).T], axis=1)),
    )
    sgu = np.asarray(sgu, f32)
    sdn = np.asarray(sdn, f32)
    w13 = np.asarray(w13, f32)
    w2 = np.asarray(w2, f32)
    w13a = np.empty((NE, H, 2816), f32)
    w2a = np.empty((NE, 1408, H), f32)
    for v in range(4):
        w13a[v, :, 0:1408] = sgu[1408 * v:1408 * (v + 1), :].T
        w13a[v, :, 1408:2816] = sgu[5632 + 1408 * v:5632 + 1408 * (v + 1), :].T
        w2a[v] = sdn[:, 1408 * v:1408 * (v + 1)].T
    for e in range(16):
        w13a[4 + e] = w13[e].T
        w2a[4 + e] = w2[e].T
    d["w13a"] = np.ascontiguousarray(
        w13a.reshape(NE, 16, 128, 22, 128).transpose(0, 3, 2, 1, 4))
    d["w2a"] = np.ascontiguousarray(w2a)
    d["wqk"] = np.ascontiguousarray(
        wqkv.reshape(16, 128, 48, 128).transpose(2, 1, 0, 3))
    jl = np.arange(128, dtype=f32)[:, None]
    il = np.arange(512, dtype=f32)[None, :]
    d4 = np.empty((128, 2048), f32)
    m4 = np.empty((128, 2048), f32)
    for m in range(4):
        dd = jl + 128 * m - il
        d4[:, 512 * m:512 * (m + 1)] = np.where(dd <= 0, dd, NEGD)
        m4[:, 512 * m:512 * (m + 1)] = np.where(dd <= 0, 0.0, NEGD)
    d["d4"] = np.ascontiguousarray(d4)
    d["m4"] = np.ascontiguousarray(m4)
    return d


def _prep_core(x, b, r):
    f32 = np.float32
    xb = np.asarray(x[b], f32)
    pmv = np.zeros((128, 32), f32)
    if r == 0:
        xc = np.concatenate([np.zeros((OWN0, H), f32), xb[:OWN0]], axis=0)
        pmv[:, 0:16] = -1.0e30
    else:
        xc = xb
    return np.ascontiguousarray(xc), pmv


def kernel(L0_attn_x, L0_attn_Wqkv, L0_attn_bqkv, L0_attn_Wo, L0_attn_bo,
           L0_ffn_gate_w, L0_ffn_sgate_w, L0_ffn_w13, L0_ffn_w2,
           L0_ffn_shared_gate_up_w, L0_ffn_shared_down_w):
    shared = _prep_shared(L0_attn_Wqkv, L0_attn_bqkv, L0_attn_Wo, L0_attn_bo,
                          L0_ffn_gate_w, L0_ffn_sgate_w, L0_ffn_w13, L0_ffn_w2,
                          L0_ffn_shared_gate_up_w, L0_ffn_shared_down_w)
    nc = build_nc(debug=DEBUG)
    in_maps = []
    for c in range(8):
        b, r = c // 2, c % 2
        xc, pmv = _prep_core(L0_attn_x, b, r)
        m = dict(shared)
        m["x"] = xc
        m["pm"] = pmv
        in_maps.append(m)
    res = run_bass_kernel_spmd(nc, in_maps, core_ids=list(range(8)))
    out = np.empty((4, SEQ, H), np.float32)
    for c in range(8):
        b, r = c // 2, c % 2
        out[b, OWN0 * r:OWN0 * (r + 1), :] = res.results[c]["out"]
    return out
